# version 01 (final)
# baseline (speedup 1.0000x reference)
import numpy as np
from contextlib import ExitStack

import concourse.bass as bass
import concourse.mybir as mybir
from concourse.bass_utils import run_bass_kernel_spmd

F32 = mybir.dt.float32
BF16 = mybir.dt.bfloat16
U32 = mybir.dt.uint32
AF = mybir.ActivationFunctionType
ALU = mybir.AluOpType
AX = mybir.AxisListType

P = 128
NCORES = 8
T = 2048
D = 1024
DC = 8
E0, H0 = 4, 2730
H0P = 2816
HC0 = 22
E1, F1 = 16, 4096
FC1 = 32
CAP = 1024
NH, DH, M0 = 16, 64, 11

N_SUB = 4
DEBUG_STAGE = None
TRUNC = None
NTB = 16
SKIP_CLIP = False
SKIP_VPROJ = False
SKIP_KEVAC = False
CMB_N = 16
CMB_MODE = "full"


class Ctr:
    def __init__(self, handle, step=1):
        self.h = handle
        self.v = 0
        self.step = step

    def inc(self, n=1):
        self.v += self.step * n
        return self.v


class Ctx:
    pass


def scope_fence(g, eng):
    eng.wait_ge(g.sPE.h, g.sPE.v)
    eng.wait_ge(g.sSC.h, g.sSC.v)
    eng.wait_ge(g.sDV.h, g.sDV.v)
    eng.wait_ge(g.qA.h, g.qA.v)



def _pad_ax(a, axis, to):
    pad = [(0, 0)] * a.ndim
    pad[axis] = (0, to - a.shape[axis])
    return np.pad(a, pad)


def prep_shared(inputs):
    f = np.float32
    out = {}
    gw = _pad_ax(np.asarray(inputs["L0_ffn_gate_w"], f), 2, H0P)
    uw = _pad_ax(np.asarray(inputs["L0_ffn_up_w"], f), 2, H0P)
    dw = _pad_ax(np.asarray(inputs["L0_ffn_down_w"], f), 1, H0P)
    out["gw_t"] = np.ascontiguousarray(
        gw.reshape(E0, DC, P, HC0, P).transpose(0, 3, 2, 1, 4)).reshape(E0, HC0, P, DC * P)
    out["uw_t"] = np.ascontiguousarray(
        uw.reshape(E0, DC, P, HC0, P).transpose(0, 3, 2, 1, 4)).reshape(E0, HC0, P, DC * P)
    out["dw_t"] = np.ascontiguousarray(
        dw.reshape(E0, HC0, P, DC, P).transpose(0, 3, 2, 1, 4)).reshape(E0, DC, P, HC0 * P)
    out["gb_r"] = _pad_ax(np.asarray(inputs["L0_ffn_gate_b"], f), 1, H0P).reshape(1, E0 * H0P)
    out["ub_r"] = _pad_ax(np.asarray(inputs["L0_ffn_up_b"], f), 1, H0P).reshape(1, E0 * H0P)
    out["db_r"] = np.asarray(inputs["L0_ffn_down_b"], f).reshape(1, E0 * D)
    rw = np.asarray(inputs["L0_ffn_router_w"], f)
    out["rw_t"] = np.ascontiguousarray(rw.reshape(DC, P, E0).transpose(1, 0, 2)).reshape(P, DC * E0)
    w_in = np.asarray(inputs["L0_attn_w_in"], f)
    wq, wk, wv = w_in[:D], w_in[D:2 * D], w_in[2 * D:]
    out["wq_t"] = np.ascontiguousarray(
        wq.reshape(DC, P, DC, P).transpose(0, 3, 2, 1)).reshape(DC, P, DC * P)
    out["wk_r"] = np.ascontiguousarray(wk.T.reshape(DC, P, D))
    out["wv_r"] = np.ascontiguousarray(wv.T.reshape(DC, P, D))
    w_out = np.asarray(inputs["L0_attn_w_out"], f)
    out["wo_t"] = np.ascontiguousarray(
        w_out.reshape(DC, P, DC, P).transpose(0, 3, 2, 1)).reshape(DC, P, DC * P)
    beta = np.asarray(inputs["L0_attn_beta"], f)
    bb = np.zeros((NH // 2, P, 704), f)
    r = np.arange(P)
    for hp in range(NH // 2):
        for m in range(M0):
            bb[hp, :, m * DH:(m + 1) * DH] = beta[2 * hp + r // DH, m][:, None]
    out["beta_bc"] = bb
    QK = np.asarray(inputs["L1_attn_QK"], f)
    out["qk_t"] = np.ascontiguousarray(
        QK.reshape(DC, P, DC, P).transpose(2, 1, 0, 3)).reshape(DC, P, DC * P)
    VO = np.asarray(inputs["L1_attn_VO"], f)
    out["vo_t"] = np.ascontiguousarray(
        VO.reshape(DC, P, DC, P).transpose(2, 1, 0, 3)).reshape(DC, P, DC * P)
    Wg = np.asarray(inputs["L1_ffn_Wg"], f)
    out["wg_t"] = np.ascontiguousarray(Wg.reshape(DC, P, E1).transpose(1, 0, 2)).reshape(P, DC * E1)
    out["ident_f"] = np.eye(P, dtype=f)
    out["ident_b"] = np.eye(P, dtype=f)
    out["lt_s"] = np.ascontiguousarray(np.tril(np.ones((P, P), f), -1).T)
    return out


def prep_core(inputs, shared, c):
    f = np.float32
    x = np.asarray(inputs["L0_attn_x"], f).reshape(NCORES * T, D)
    m = dict(shared)
    m["x_fm"] = np.ascontiguousarray(x[c * T:(c + 1) * T].T)
    w1 = np.asarray(inputs["L1_ffn_w1"], f)[2 * c:2 * c + 2]
    w2 = np.asarray(inputs["L1_ffn_w2"], f)[2 * c:2 * c + 2]
    m["w1_t"] = np.ascontiguousarray(
        w1.reshape(2, DC, P, FC1, P).transpose(0, 3, 2, 1, 4)).reshape(2, FC1, P, DC * P)
    m["w2_t"] = np.ascontiguousarray(
        w2.reshape(2, FC1, P, DC, P).transpose(0, 3, 2, 1, 4)).reshape(2, DC, P, FC1 * P)
    cm = np.zeros((8, 1), f)
    cm[:c] = 1.0
    m["coremask"] = cm
    m["rotvec"] = np.full((P, 1), float(E1 * CAP - 2 * c * CAP), f)
    return m



def build_program(n_sub):
    nc = bass.Bass("TRN2", target_bir_lowering=False)
    es = ExitStack()
    g = Ctx()
    g.nc = nc
    g.es = es
    g._uid = 0
    g.es_dbg = es

    def uid(s):
        g._uid += 1
        return f"{s}{g._uid}"
    g.uid = uid

    dr = {}

    def din(name, shape, dtype=F32):
        dr[name] = nc.dram_tensor(name, list(shape), dtype, kind="ExternalInput").ap()

    din("x_fm", (D, T))
    din("gw_t", (E0, HC0, P, DC * P)); din("uw_t", (E0, HC0, P, DC * P))
    din("dw_t", (E0, DC, P, HC0 * P))
    din("gb_r", (1, E0 * H0P)); din("ub_r", (1, E0 * H0P)); din("db_r", (1, E0 * D))
    din("rw_t", (P, DC * E0))
    din("wq_t", (DC, P, DC * P)); din("wk_r", (DC, P, D)); din("wv_r", (DC, P, D))
    din("wo_t", (DC, P, DC * P)); din("beta_bc", (NH // 2, P, 704))
    din("qk_t", (DC, P, DC * P)); din("vo_t", (DC, P, DC * P))
    din("wg_t", (P, DC * E1))
    din("w1_t", (2, FC1, P, DC * P)); din("w2_t", (2, DC, P, FC1 * P))
    din("ident_f", (P, P)); din("ident_b", (P, P)); din("lt_s", (P, P))
    din("coremask", (8, 1)); din("rotvec", (P, 1))
    g.out_fm = nc.dram_tensor("out_fm", [D, T], F32, kind="ExternalOutput").ap()
    g.dr = dr

    g.qs_dram = nc.dram_tensor("qs_dram", [DC * P, T], F32)
    g.lnab_dram = nc.dram_tensor("lnab_dram", [2, T], F32)
    g.kv_cin = nc.dram_tensor("kv_cin", [NH // 2 * P, 704], F32)
    g.kv_cout = nc.dram_tensor("kv_cout", [NH // 2 * P, 704], F32)
    g.ag_in_tm = nc.dram_tensor("ag_in_tm", [T, D], BF16)
    g.ag_out_tm8 = nc.dram_tensor("ag_out_tm8", [NCORES * T, D], BF16)
    g.ag_out_tm4 = nc.dram_tensor("ag_out_tm4", [4 * T, D], BF16)
    g.ag_in_fm = nc.dram_tensor("ag_in_fm", [D, T], BF16)
    g.ag_out_fm = nc.dram_tensor("ag_out_fm", [4 * D, T], BF16)
    g.rt_in = nc.dram_tensor("rt_in", [1, E1], F32)
    g.rt_out = nc.dram_tensor("rt_out", [NCORES, E1], F32)
    g.gx_in = nc.dram_tensor("gx_in", [P, 16], F32)
    g.gx_out = nc.dram_tensor("gx_out", [NCORES * P, 16], F32)
    g.s2t = nc.dram_tensor("s2t", [E1 * CAP, 2], U32)
    g.eo_in = nc.dram_tensor("eo_in", [2 * CAP, D], BF16)
    g.eo_out = nc.dram_tensor("eo_out", [NCORES * 2 * CAP, D], BF16)

    def sem(name, step=1):
        return Ctr(es.enter_context(nc.semaphore(name)), step)

    g.bndreg = es.enter_context(nc.gpsimd.register("bndreg"))
    g.qS = sem("qS", 16)
    g.qG = sem("qG", 16)
    g.qA = sem("qA", 16)
    g.sPE = sem("sPE")
    g.sDV = sem("sDV")
    g.sSC = sem("sSC")
    g.sGP = sem("sGP")
    g.sCC = sem("sCC")

    def sb(name, shape, dtype):
        t = es.enter_context(nc.sbuf_tensor(name, list(shape), dtype))
        setattr(g, name, t)
        return t

    sb("X", (P, DC * T), F32)
    sb("XT", (P, DC * T), BF16)
    sb("stat", (1, 4 * 512), F32)
    sb("onescol_f", (P, 1), F32)
    sb("onesrow_f", (1, P), F32)
    sb("onesrow_b", (1, 512), BF16)
    sb("ident_fs", (P, P), F32)
    sb("ident_bs", (P, P), BF16)
    sb("lt_ss", (P, P), F32)
    sb("ev_scr", (P, 512), F32)
    g.sbh = sb
    if DEBUG_STAGE in ("wbc", "rt1"):
        sb("dbg_wbc", (P, E0 * T), F32)

    block = es.enter_context(nc.Block())
    g.block = block

    def _(sync):
        for dc in range(DC):
            sync.dma_start(out=g.X[:, dc * T:(dc + 1) * T],
                           in_=dr["x_fm"][dc * P:(dc + 1) * P, :]).then_inc(g.qS.h, 16)
            g.qS.inc()
        sync.dma_start(out=g.ident_fs[:], in_=dr["ident_f"][:, :]).then_inc(g.qS.h, 16)
        g.qS.inc()
        sync.dma_start(out=g.lt_ss[:], in_=dr["lt_s"][:, :]).then_inc(g.qS.h, 16)
        g.qS.inc()
    block.sync(_)

    def _(gp):
        gp.reg_mov(g.bndreg, E1 * CAP - 1)
        gp.dma_start(out=g.ident_bs[:], in_=dr["ident_b"][:, :]).then_inc(g.qG.h, 16)
        g.qG.inc(1)
        gp.memset(g.onescol_f[:], 1.0)
        gp.memset(g.onesrow_f[:], 1.0)
        gp.memset(g.onesrow_b[:], 1.0).then_inc(g.sGP.h, 1)
        g.sGP.inc()
    block.gpsimd(_)

    g.init_qS = g.qS.v
    g.init_qG = g.qG.v
    g.init_sGP = g.sGP.v

    seq = ["l0ffn", "l0attn", "l0ffn", "l1ffn", "l1attn", "l1ffn"]
    emitters = {"l0ffn": emit_l0ffn, "l0attn": emit_l0attn,
                "l1ffn": emit_l1ffn, "l1attn": emit_l1attn}
    for i, kind in enumerate(seq[:n_sub]):
        emit_ln(g, first=(i == 0), store_ab=(kind == 'l1ffn'))
        emitters[kind](g)

    src_t = g.XT if DEBUG_STAGE in ("ln", "l0a_ao") else g.X
    if DEBUG_STAGE in ("wbc", "rt1"):
        src_t = g.dbg_wbc

    def _(gp):
        gp.wait_ge(g.sDV.h, g.sDV.v)
        gp.wait_ge(g.sPE.h, g.sPE.v)
        gp.wait_ge(g.sSC.h, g.sSC.v)
        if DEBUG_STAGE not in ("l0a_kv", "l0a_vt", "l1rt"):
            for dc in range(DC):
                if DEBUG_STAGE in ("wbc", "rt1") and dc >= 4:
                    break
                gp.dma_start(out=g.out_fm[dc * P:(dc + 1) * P, :],
                             in_=src_t[:, dc * T:(dc + 1) * T]).then_inc(g.qG.h, 16)
                g.qG.inc()
        gp.wait_ge(g.qG.h, g.qG.v)
    block.gpsimd(_)

    es.close()
    return nc



def emit_ln(g, first=False, store_ab=False):
    nc, block = g.nc, g.block
    x_ready_dv = g.sDV.v
    x_ready_pe = g.sPE.v
    with nc.psum_tensor(g.uid("ps_ln_s"), [P, 512], F32) as ps_s, \
         nc.psum_tensor(g.uid("ps_ln_q"), [P, 512], F32) as ps_q, \
         nc.psum_tensor(g.uid("ps_ln_a"), [P, 512], F32) as ps_a, \
         nc.psum_tensor(g.uid("ps_ln_b"), [P, 512], F32) as ps_b, \
         nc.sbuf_tensor(g.uid("sq0_"), [P, 512], F32) as sq0, \
         nc.sbuf_tensor(g.uid("sq1_"), [P, 512], F32) as sq1:
        mean = g.stat[0:1, 0:512]
        msq = g.stat[0:1, 512:1024]
        tmp = g.stat[0:1, 1024:1536]
        arow = g.stat[0:1, 1536:2048]
        for tc in range(4):
            t0 = tc * 512
            base_sc = g.sSC.v
            base_pe = g.sPE.v

            def _(scalar, t0=t0, base_pe=base_pe, tc=tc):
                if first:
                    scalar.wait_ge(g.qS.h, g.init_qS)
                scalar.wait_ge(g.sDV.h, x_ready_dv)
                if tc == 0:
                    scalar.wait_ge(g.sPE.h, x_ready_pe)
                for dc in range(DC):
                    sq = (sq0, sq1)[dc % 2]
                    if dc >= 2:
                        scalar.wait_ge(g.sPE.h, base_pe + (dc - 1))
                    scalar.activation(sq[:], g.X[:, dc * T + t0:dc * T + t0 + 512],
                                      AF.Square).then_inc(g.sSC.h, 1)
                    g.sSC.inc()
            block.scalar(_)

            def _(t, t0=t0, base_sc=base_sc):
                if first:
                    t.wait_ge(g.qS.h, g.init_qS)
                    t.wait_ge(g.sGP.h, g.init_sGP)
                t.wait_ge(g.sDV.h, x_ready_dv)
                t.wait_ge(g.sPE.h, x_ready_pe)
                for dc in range(DC):
                    sq = (sq0, sq1)[dc % 2]
                    t.wait_ge(g.sSC.h, base_sc + dc + 1)
                    t.matmul(ps_s[0:1, :], g.onescol_f[:],
                             g.X[:, dc * T + t0:dc * T + t0 + 512],
                             start=(dc == 0), stop=(dc == DC - 1))
                    t.matmul(ps_q[0:1, :], g.onescol_f[:], sq[:],
                             start=(dc == 0), stop=(dc == DC - 1)).then_inc(g.sPE.h, 1)
                    g.sPE.inc()
            block.tensor(_)
            pe_stats = g.sPE.v

            def _(v):
                v.wait_ge(g.sPE.h, pe_stats)
                v.tensor_scalar(mean, ps_s[0:1, :], 1.0 / D, None, ALU.mult)
                v.tensor_scalar(msq, ps_q[0:1, :], 1.0 / D, None, ALU.mult)
                v.drain()
                v.tensor_tensor(tmp, mean, mean, ALU.mult)
                v.drain()
                v.tensor_tensor(msq, msq, tmp, ALU.subtract)
                v.drain()
                v.tensor_scalar(msq, msq, 1e-5, None, ALU.add).then_inc(g.sDV.h, 1)
                g.sDV.inc()
            block.vector(_)
            dv_var = g.sDV.v

            def _(scalar):
                scalar.wait_ge(g.sDV.h, dv_var)
                scalar.activation(tmp, msq, AF.Ln)
                scalar.drain()
                scalar.activation(arow, tmp, AF.Exp, scale=-0.5).then_inc(g.sSC.h, 1)
                g.sSC.inc()
            block.scalar(_)
            sc_a = g.sSC.v

            def _(v):
                v.wait_ge(g.sSC.h, sc_a)
                v.scalar_tensor_tensor(tmp, mean, -1.0, arow, ALU.mult,
                                       ALU.mult).then_inc(g.sDV.h, 1)
                g.sDV.inc()
            block.vector(_)
            dv_b = g.sDV.v

            if store_ab:
                def _(scalar, t0=t0):
                    scalar.wait_ge(g.sDV.h, dv_b)
                    scalar.dma_start(out=g.lnab_dram[0:1, t0:t0 + 512].opt(),
                                     in_=arow).then_inc(g.qA.h, 16)
                    scalar.dma_start(out=g.lnab_dram[1:2, t0:t0 + 512].opt(),
                                     in_=tmp).then_inc(g.qA.h, 16)
                    g.qA.inc(2)
                block.scalar(_)

            def _(t):
                t.wait_ge(g.sDV.h, dv_b)
                t.matmul(ps_a[:, :], g.onesrow_f[:], arow, start=True, stop=True)
                t.matmul(ps_b[:, :], g.onesrow_f[:], tmp,
                         start=True, stop=True).then_inc(g.sPE.h, 1)
                g.sPE.inc()
            block.tensor(_)
            pe_ab = g.sPE.v

            def _(v, t0=t0):
                v.wait_ge(g.sPE.h, pe_ab)
                for dc in range(DC):
                    v.tensor_tensor(g.ev_scr[:], g.X[:, dc * T + t0:dc * T + t0 + 512],
                                    ps_a[:, :], ALU.mult)
                    v.drain()
                    mm = v.tensor_tensor(g.XT[:, dc * T + t0:dc * T + t0 + 512],
                                         g.ev_scr[:], ps_b[:, :], ALU.add)
                    v.drain()
                mm.then_inc(g.sDV.h, 1)
                g.sDV.inc()
            block.vector(_)
            dv_done = g.sDV.v

            def _(t):
                t.wait_ge(g.sDV.h, dv_done)
            block.tensor(_)
    g.ln_done_dv = g.sDV.v



def emit_l0ffn(g):
    nc, block = g.nc, g.block
    dr = g.dr
    ln_dv = g.ln_done_dv

    with nc.sbuf_tensor(g.uid("w_bc"), [P, E0 * T], BF16) as w_bc, \
         nc.sbuf_tensor(g.uid("router"), [P, DC * E0], BF16) as router_sb, \
         nc.sbuf_tensor(g.uid("gbe"), [1, H0P], BF16) as gb_sb, \
         nc.sbuf_tensor(g.uid("ube"), [1, H0P], BF16) as ub_sb, \
         nc.sbuf_tensor(g.uid("dbe"), [1, D], BF16) as db_sb:

        with nc.psum_tensor(g.uid("ps_r"), [P, 512], F32) as ps_r, \
             nc.psum_tensor(g.uid("ps_rt"), [P, 512], F32) as ps_rt, \
             nc.sbuf_tensor(g.uid("lg4"), [E0, T], F32) as lg4, \
             nc.sbuf_tensor(g.uid("ltm"), [P, 16 * E0], F32) as ltm, \
             nc.sbuf_tensor(g.uid("ltm2"), [P, 16 * E0], F32) as ltm2, \
             nc.sbuf_tensor(g.uid("ltm3"), [P, 16 * E0], F32) as ltm3, \
             nc.sbuf_tensor(g.uid("ltm4"), [P, 16 * E0], F32) as ltm4, \
             nc.sbuf_tensor(g.uid("rmx"), [P, 16], F32) as rmx, \
             nc.sbuf_tensor(g.uid("rmx2"), [P, 16], F32) as rmx2, \
             nc.sbuf_tensor(g.uid("wrow"), [1, T], BF16) as wrow:

            def _(gp):
                scope_fence(g, gp)
                gp.dma_start(out=router_sb[:], in_=dr["rw_t"][:, :]).then_inc(g.qG.h, 16)
                g.qG.inc(1)
            block.gpsimd(_)
            qg_rt = g.qG.v

            for tc in range(4):
                t0 = tc * 512
                dv_prev_evac = g.sDV.v

                def _(t, t0=t0, tc=tc, dv_prev_evac=dv_prev_evac):
                    if tc == 0:
                        t.wait_ge(g.qG.h, qg_rt)
                        t.wait_ge(g.sDV.h, ln_dv)
                    else:
                        t.wait_ge(g.sDV.h, dv_prev_evac)
                    for dc in range(DC):
                        mm = t.matmul(ps_r[0:E0, :], router_sb[:, dc * E0:(dc + 1) * E0],
                                      g.XT[:, dc * T + t0:dc * T + t0 + 512],
                                      start=(dc == 0), stop=(dc == DC - 1))
                    mm.then_inc(g.sPE.h, 1)
                    g.sPE.inc()
                block.tensor(_)
                pe_l = g.sPE.v

                def _(v, t0=t0, pe_l=pe_l):
                    v.wait_ge(g.sPE.h, pe_l)
                    v.tensor_copy(lg4[:, t0:t0 + 512], ps_r[0:E0, :]).then_inc(g.sDV.h, 1)
                    g.sDV.inc()
                block.vector(_)
            dv_lg = g.sDV.v

            def _(t):
                t.wait_ge(g.sDV.h, dv_lg)
                for blk in range(16):
                    mm = t.transpose(ps_rt[0:P, blk * E0:(blk + 1) * E0],
                                     lg4[:, blk * P:(blk + 1) * P], g.ident_fs[0:E0, 0:E0])
                mm.then_inc(g.sPE.h, 1)
                g.sPE.inc()
            block.tensor(_)
            pe_t = g.sPE.v

            def _(v):
                v.wait_ge(g.sPE.h, pe_t)
                v.tensor_copy(ltm[:], ps_rt[:, 0:16 * E0])
                v.drain()
                v.tensor_reduce(rmx[:], ltm[:].rearrange("p (b e) -> p b e", e=E0),
                                axis=AX.X, op=ALU.max)
                v.drain()
                for blk in range(16):
                    v.tensor_scalar(ltm2[:, blk * E0:(blk + 1) * E0],
                                    ltm[:, blk * E0:(blk + 1) * E0],
                                    rmx[:, blk:blk + 1], None, ALU.subtract)
                v.drain()
                v.tensor_scalar(ltm3[:], ltm2[:], 0.0, None, ALU.is_ge)
                v.drain()
                v.scalar_tensor_tensor(ltm4[:], ltm3[:], -1e30, ltm[:], ALU.mult, ALU.add)
                v.drain()
                v.tensor_reduce(rmx2[:], ltm4[:].rearrange("p (b e) -> p b e", e=E0),
                                axis=AX.X, op=ALU.max).then_inc(g.sDV.h, 1)
                g.sDV.inc()
            block.vector(_)
            dv_m = g.sDV.v

            if DEBUG_STAGE == "rt1":
                def _(v):
                    v.wait_ge(g.sDV.h, dv_m)
                    v.tensor_copy(g.dbg_wbc[:, 0:64], ltm[:])
                    v.tensor_copy(g.dbg_wbc[:, 64:128], ltm2[:])
                    v.tensor_copy(g.dbg_wbc[:, 128:192], ltm3[:])
                    v.tensor_copy(g.dbg_wbc[:, 192:208], rmx[:])
                    mm = v.tensor_copy(g.dbg_wbc[:, 208:224], rmx2[:])
                    mm.then_inc(g.sDV.h, 1)
                    g.sDV.inc()
                block.vector(_)

            def _(scalar):
                scalar.wait_ge(g.sDV.h, g.sDV.v)
                scalar.activation(ltm2[:], ltm2[:], AF.Exp).then_inc(g.sSC.h, 1)
                g.sSC.inc()
            block.scalar(_)
            sc_e = g.sSC.v

            def _(v):
                v.wait_ge(g.sSC.h, sc_e)
                for blk in range(16):
                    v.tensor_scalar(ltm3[:, blk * E0:(blk + 1) * E0],
                                    ltm[:, blk * E0:(blk + 1) * E0],
                                    rmx2[:, blk:blk + 1], None, ALU.is_ge)
                v.drain()
                v.tensor_tensor(ltm4[:], ltm2[:], ltm3[:], ALU.mult)
                v.drain()
                v.tensor_reduce(rmx[:], ltm4[:].rearrange("p (b e) -> p b e", e=E0),
                                axis=AX.X, op=ALU.add)
                v.drain()
                v.reciprocal(rmx2[:], rmx[:])
                v.drain()
                for blk in range(16):
                    mm = v.tensor_scalar(ltm[:, blk * E0:(blk + 1) * E0],
                                         ltm4[:, blk * E0:(blk + 1) * E0],
                                         rmx2[:, blk:blk + 1], 0.5, ALU.mult, ALU.mult)
                mm.then_inc(g.sDV.h, 1)
                g.sDV.inc()
            block.vector(_)
            dv_w = g.sDV.v

            if DEBUG_STAGE == "rt1":
                def _(v):
                    v.wait_ge(g.sDV.h, dv_w)
                    v.tensor_copy(g.dbg_wbc[:, 256:320], ltm[:])
                    mm = v.tensor_copy(g.dbg_wbc[:, 320:384], ltm3[:])
                    mm.then_inc(g.sDV.h, 1)
                    g.sDV.inc()
                block.vector(_)

            for q in range(4):
                dv_need = dv_w if q == 0 else g.sDV.v

                def _(t, q=q, dv_need=dv_need):
                    t.wait_ge(g.sDV.h, dv_need)
                    for j in range(4):
                        blk = q * 4 + j
                        mm = t.transpose(ps_rt[0:E0, j * P:(j + 1) * P],
                                         ltm[:, blk * E0:(blk + 1) * E0], g.ident_fs[:, :])
                    mm.then_inc(g.sPE.h, 1)
                    g.sPE.inc()
                block.tensor(_)
                pe_q = g.sPE.v

                def _(v, q=q, pe_q=pe_q):
                    v.wait_ge(g.sPE.h, pe_q)
                    v.tensor_copy(lg4[:, q * 512:(q + 1) * 512],
                                  ps_rt[0:E0, :]).then_inc(g.sDV.h, 1)
                    g.sDV.inc()
                block.vector(_)
            dv_wfm = g.sDV.v

            for e in range(E0):
                pe_prev_bc = g.sPE.v

                def _(gp, e=e, pe_prev_bc=pe_prev_bc):
                    if e == 0:
                        gp.wait_ge(g.sDV.h, dv_wfm)
                    else:
                        gp.wait_ge(g.sPE.h, pe_prev_bc)
                    gp.dma_start(out=wrow[0:1, :],
                                 in_=lg4[e:e + 1, :]).then_inc(g.qG.h, 16)
                    g.qG.inc()
                block.gpsimd(_)
                qg_wr = g.qG.v

                for tc in range(4):
                    dv_prev = g.sDV.v

                    def _(t, e=e, tc=tc, dv_prev=dv_prev):
                        if tc == 0:
                            t.wait_ge(g.qG.h, qg_wr)
                            t.wait_ge(g.sGP.h, g.init_sGP)
                        if not (e == 0 and tc == 0):
                            t.wait_ge(g.sDV.h, dv_prev)
                        t.matmul(ps_r[:, :], g.onesrow_b[0:1, 0:P],
                                 wrow[0:1, tc * 512:(tc + 1) * 512],
                                 start=True, stop=True).then_inc(g.sPE.h, 1)
                        g.sPE.inc()
                    block.tensor(_)
                    pe_b = g.sPE.v

                    def _(v, e=e, tc=tc, pe_b=pe_b):
                        v.wait_ge(g.sPE.h, pe_b)
                        v.tensor_copy(w_bc[:, e * T + tc * 512:e * T + tc * 512 + 512],
                                      ps_r[:, :]).then_inc(g.sDV.h, 1)
                        g.sDV.inc()
                    block.vector(_)
        dv_routing_done = g.sDV.v
        if DEBUG_STAGE == "wbc":
            def _(v):
                v.wait_ge(g.sDV.h, dv_routing_done)
                for e in range(E0):
                    mm = v.tensor_copy(g.dbg_wbc[:, e * T:(e + 1) * T],
                                       w_bc[:, e * T:(e + 1) * T])
                mm.then_inc(g.sDV.h, 1)
                g.sDV.inc()
            block.vector(_)
            return
        if DEBUG_STAGE == "rt1":
            def _(v):
                v.wait_ge(g.sDV.h, dv_routing_done)
                for e in range(E0):
                    mm = v.tensor_copy(g.dbg_wbc[:, 1024 + e * T:1024 + e * T + 512],
                                       w_bc[:, e * T:e * T + 512])
                mm.then_inc(g.sDV.h, 1)
                g.sDV.inc()
            block.vector(_)
            return

        with nc.psum_tensor(g.uid("ps_g0"), [P, 512], F32) as ps_g0, \
             nc.psum_tensor(g.uid("ps_g1"), [P, 512], F32) as ps_g1, \
             nc.psum_tensor(g.uid("ps_u0"), [P, 512], F32) as ps_u0, \
             nc.psum_tensor(g.uid("ps_u1"), [P, 512], F32) as ps_u1, \
             nc.psum_tensor(g.uid("ps_d0"), [P, 512], F32) as ps_d0, \
             nc.psum_tensor(g.uid("ps_d1"), [P, 512], F32) as ps_d1, \
             nc.sbuf_tensor(g.uid("gwt"), [P, 2 * DC * P], BF16) as gwt, \
             nc.sbuf_tensor(g.uid("uwt"), [P, 2 * DC * P], BF16) as uwt, \
             nc.sbuf_tensor(g.uid("dwt"), [P, 2 * HC0 * P], BF16) as dwt, \
             nc.sbuf_tensor(g.uid("hid"), [P, HC0 * 1024], BF16) as hid, \
             nc.sbuf_tensor(g.uid("sg"), [P, 2 * 512], BF16) as sg_scr:
            ps_g = (ps_g0, ps_g1)
            ps_u = (ps_u0, ps_u1)
            ps_d = (ps_d0, ps_d1)
            gw_load_val = {}
            dw_load_val = {}
            pe_after_gw = {}
            pe_after_dw = {}
            dv_after_gu = {}
            dv_after_dn = {}
            pe_after_expert = {}
            bias_load_val = {}

            for half in range(2):
                for e in range(E0):
                    he = half * E0 + e
                    pb = he % 2

                    def _(gp, e=e, he=he):
                        if he == 0:
                            scope_fence(g, gp)
                            gp.wait_ge(g.sDV.h, dv_routing_done)
                        else:
                            gp.wait_ge(g.sPE.h, pe_after_expert[he - 1])
                        gp.dma_start(out=gb_sb[0:1, :],
                                     in_=dr["gb_r"][0:1, e * H0P:(e + 1) * H0P]).then_inc(g.qG.h, 16)
                        gp.dma_start(out=ub_sb[0:1, :],
                                     in_=dr["ub_r"][0:1, e * H0P:(e + 1) * H0P]).then_inc(g.qG.h, 16)
                        gp.dma_start(out=db_sb[0:1, :],
                                     in_=dr["db_r"][0:1, e * D:(e + 1) * D]).then_inc(g.qG.h, 16)
                        g.qG.inc(3)
                        bias_load_val[he] = g.qG.v
                    block.gpsimd(_)

                    for h in range(HC0):
                        i = half * E0 * HC0 + e * HC0 + h
                        par = i % 2

                        def _(gp, e=e, h=h, i=i, par=par):
                            if i >= 2:
                                gp.wait_ge(g.sPE.h, pe_after_gw[i - 2])
                            gp.dma_start(out=gwt[:, par * DC * P:(par + 1) * DC * P],
                                         in_=dr["gw_t"][e, h, :, :]).then_inc(g.qG.h, 16)
                            gp.dma_start(out=uwt[:, par * DC * P:(par + 1) * DC * P],
                                         in_=dr["uw_t"][e, h, :, :]).then_inc(g.qG.h, 16)
                            g.qG.inc(2)
                            gw_load_val[i] = g.qG.v
                        block.gpsimd(_)

                        def _(t, e=e, h=h, i=i, par=par, half=half, he=he, pb=pb):
                            t.wait_ge(g.qG.h, gw_load_val[i])
                            if h == 0:
                                t.wait_ge(g.qG.h, bias_load_val[he])
                            for tc2 in range(2):
                                n = 2 * i + tc2
                                if n >= 2:
                                    t.wait_ge(g.sDV.h, dv_after_gu[n - 2])
                                t0 = half * 1024 + tc2 * 512
                                pg, pu = ps_g[n % 2], ps_u[n % 2]
                                t.matmul(pg[:, :],
                                         gb_sb[0:1, h * P:(h + 1) * P],
                                         g.onesrow_b[0:1, :], start=True, stop=False)
                                for dc in range(DC):
                                    t.matmul(pg[:, :],
                                             gwt[:, par * DC * P + dc * P:par * DC * P + (dc + 1) * P],
                                             g.XT[:, dc * T + t0:dc * T + t0 + 512],
                                             start=False, stop=(dc == DC - 1))
                                t.matmul(pu[:, :],
                                         ub_sb[0:1, h * P:(h + 1) * P],
                                         g.onesrow_b[0:1, :], start=True, stop=False)
                                for dc in range(DC):
                                    mm = t.matmul(pu[:, :],
                                                  uwt[:, par * DC * P + dc * P:par * DC * P + (dc + 1) * P],
                                                  g.XT[:, dc * T + t0:dc * T + t0 + 512],
                                                  start=False, stop=(dc == DC - 1))
                                mm.then_inc(g.sPE.h, 1)
                                g.sPE.inc()
                            pe_after_gw[i] = g.sPE.v
                        block.tensor(_)
                        pe_gu = g.sPE.v

                        for tc2 in range(2):
                            n = 2 * i + tc2
                            pg, pu = ps_g[n % 2], ps_u[n % 2]
                            pe_need = pe_gu - (1 - tc2)
                            sgs = sg_scr[:, (n % 2) * 512:(n % 2 + 1) * 512]

                            def _(scalar, pg=pg, pe_need=pe_need, sgs=sgs, n=n):
                                scalar.wait_ge(g.sPE.h, pe_need)
                                if n >= 2:
                                    scalar.wait_ge(g.sDV.h, dv_after_gu[n - 2])
                                scalar.activation(sgs, pg[:, :], AF.Silu).then_inc(g.sSC.h, 1)
                                g.sSC.inc()
                            block.scalar(_)
                            sc_s = g.sSC.v

                            def _(v, pu=pu, h=h, tc2=tc2, sc_s=sc_s, sgs=sgs, n=n):
                                v.wait_ge(g.sSC.h, sc_s)
                                v.tensor_tensor(
                                    hid[:, h * 1024 + tc2 * 512:h * 1024 + tc2 * 512 + 512],
                                    sgs, pu[:, :], ALU.mult).then_inc(g.sDV.h, 1)
                                g.sDV.inc()
                                dv_after_gu[n] = g.sDV.v
                            block.vector(_)
                    dv_hid_done = g.sDV.v

                    for d in range(DC):
                        j = (half * E0 + e) * DC + d
                        par = j % 2

                        def _(gp, e=e, d=d, j=j, par=par):
                            if j >= 2:
                                gp.wait_ge(g.sPE.h, pe_after_dw[j - 2])
                            gp.dma_start(out=dwt[:, par * HC0 * P:(par + 1) * HC0 * P],
                                         in_=dr["dw_t"][e, d, :, :]).then_inc(g.qG.h, 16)
                            g.qG.inc()
                            dw_load_val[j] = g.qG.v
                        block.gpsimd(_)

                        def _(t, e=e, d=d, j=j, par=par, half=half, dv_hid_done=dv_hid_done, pb=pb):
                            t.wait_ge(g.qG.h, dw_load_val[j])
                            t.wait_ge(g.sDV.h, dv_hid_done)
                            for tc2 in range(2):
                                n = 2 * j + tc2
                                if n >= 2:
                                    t.wait_ge(g.sDV.h, dv_after_dn[n - 2])
                                pd = ps_d[n % 2]
                                t.matmul(pd[:, :],
                                         db_sb[0:1, d * P:(d + 1) * P],
                                         g.onesrow_b[0:1, :], start=True, stop=False)
                                for h in range(HC0):
                                    mm = t.matmul(
                                        pd[:, :],
                                        dwt[:, par * HC0 * P + h * P:par * HC0 * P + (h + 1) * P],
                                        hid[:, h * 1024 + tc2 * 512:h * 1024 + tc2 * 512 + 512],
                                        start=False, stop=(h == HC0 - 1))
                                mm.then_inc(g.sPE.h, 1)
                                g.sPE.inc()
                            pe_after_dw[j] = g.sPE.v
                            pe_after_expert[he] = g.sPE.v
                        block.tensor(_)
                        pe_dn = g.sPE.v

                        for tc2 in range(2):
                            n = 2 * j + tc2
                            pd = ps_d[n % 2]
                            pe_need = pe_dn - (1 - tc2)

                            def _(v, pd=pd, e=e, d=d, tc2=tc2, half=half,
                                  pe_need=pe_need, n=n):
                                v.wait_ge(g.sPE.h, pe_need)
                                t0 = half * 1024 + tc2 * 512
                                v.tensor_tensor(g.ev_scr[:], pd[:, :],
                                                w_bc[:, e * T + t0:e * T + t0 + 512], ALU.mult)
                                v.drain()
                                v.tensor_tensor(g.X[:, d * T + t0:d * T + t0 + 512],
                                                g.X[:, d * T + t0:d * T + t0 + 512],
                                                g.ev_scr[:], ALU.add)
                                v.drain().then_inc(g.sDV.h, 1)
                                g.sDV.inc()
                                dv_after_dn[n] = g.sDV.v
                            block.vector(_)


def emit_l0attn(g):
    nc, block = g.nc, g.block
    dr = g.dr
    ln_dv = g.ln_done_dv
    SQ2 = 1.4142135623730951

    with nc.psum_tensor(g.uid("ps_qa"), [P, 512], F32) as ps_qa, \
         nc.psum_tensor(g.uid("ps_qb"), [P, 512], F32) as ps_qb, \
         nc.sbuf_tensor(g.uid("wq"), [P, DC * D], BF16) as wq_sb, \
         nc.sbuf_tensor(g.uid("qscr"), [P, 1024], F32) as qscr, \
         nc.sbuf_tensor(g.uid("qtile"), [P, 2 * 512], F32) as qtile:
        def _(gp):
            scope_fence(g, gp)
            for oc in range(DC):
                gp.dma_start(out=wq_sb[:, oc * D:(oc + 1) * D],
                             in_=dr["wq_t"][oc, :, :]).then_inc(g.qG.h, 16)
                g.qG.inc()
        block.gpsimd(_)
        qg_wq = g.qG.v
        ps_q = (ps_qa, ps_qb)
        dv_ev = {}
        qa_st = {}
        for oc in range(DC):
            for tc in range(4):
                n = oc * 4 + tc
                t0 = tc * 512

                def _(t, oc=oc, t0=t0, n=n):
                    if n == 0:
                        t.wait_ge(g.qG.h, qg_wq)
                        t.wait_ge(g.sDV.h, ln_dv)
                    if n >= 2:
                        t.wait_ge(g.sDV.h, dv_ev[n - 2])
                    pq = ps_q[n % 2]
                    for dc in range(DC):
                        mm = t.matmul(pq[:, :],
                                      wq_sb[:, oc * D + dc * P:oc * D + (dc + 1) * P],
                                      g.XT[:, dc * T + t0:dc * T + t0 + 512],
                                      start=(dc == 0), stop=(dc == DC - 1))
                    mm.then_inc(g.sPE.h, 1)
                    g.sPE.inc()
                block.tensor(_)
                pe_q = g.sPE.v

                def _(scalar, n=n):
                    scalar.wait_ge(g.sPE.h, pe_q)
                    if n >= 2:
                        scalar.wait_ge(g.sDV.h, dv_ev[n - 2])
                    scalar.mul(qscr[:, (n % 2) * 512:(n % 2 + 1) * 512],
                               ps_q[n % 2][:, :], 0.125).then_inc(g.sSC.h, 1)
                    g.sSC.inc()
                block.scalar(_)
                sc_q = g.sSC.v

                def _(v, n=n):
                    v.wait_ge(g.sSC.h, sc_q)
                    if n >= 2:
                        v.wait_ge(g.qA.h, qa_st[n - 2])
                    v.tensor_scalar(qtile[:, (n % 2) * 512:(n % 2 + 1) * 512],
                                    qscr[:, (n % 2) * 512:(n % 2 + 1) * 512], -1.0, 1.0,
                                    ALU.max, ALU.min).then_inc(g.sDV.h, 1)
                    g.sDV.inc()
                    dv_ev[n] = g.sDV.v
                block.vector(_)
                dv_q = g.sDV.v

                def _(scalar, oc=oc, t0=t0, n=n):
                    scalar.wait_ge(g.sDV.h, dv_q)
                    scalar.dma_start(
                        out=g.qs_dram[oc * P:(oc + 1) * P, t0:t0 + 512],
                        in_=qtile[:, (n % 2) * 512:(n % 2 + 1) * 512]).then_inc(g.qA.h, 16)
                    g.qA.inc()
                    qa_st[n] = g.qA.v
                block.scalar(_)

    if TRUNC == "A":
        return
    with nc.psum_tensor(g.uid("ps_k0"), [P, 512], F32) as ps_k0, \
         nc.psum_tensor(g.uid("ps_k1"), [P, 512], F32) as ps_k1, \
         nc.psum_tensor(g.uid("ps_kva"), [P, 1024], F32) as ps_kva, \
         nc.psum_tensor(g.uid("ps_kvb"), [P, 1024], F32) as ps_kvb, \
         nc.sbuf_tensor(g.uid("wk"), [P, D], BF16) as wk_sb, \
         nc.sbuf_tensor(g.uid("wv"), [P, D], BF16) as wv_sb_x, \
         nc.sbuf_tensor(g.uid("wk2"), [P, 7 * D], BF16) as wk_sb2, \
         nc.sbuf_tensor(g.uid("wv2"), [P, 7 * D], BF16) as wv_sb2, \
         nc.sbuf_tensor(g.uid("tkb"), [P, M0 * D], BF16) as tkbuf, \
         nc.sbuf_tensor(g.uid("ktm"), [P, D], F32) as ktm, \
         nc.sbuf_tensor(g.uid("vtm"), [P, D], BF16) as vtm, \
         nc.sbuf_tensor(g.uid("w2_"), [P, 512], F32) as w2t, \
         nc.sbuf_tensor(g.uid("w3_"), [P, 512], F32) as w3t, \
         nc.sbuf_tensor(g.uid("w4_"), [P, 512], F32) as w4t, \
         nc.sbuf_tensor(g.uid("w5_"), [P, 512], F32) as w5t, \
         nc.sbuf_tensor(g.uid("sqa"), [P, 512], F32) as sqa, \
         nc.sbuf_tensor(g.uid("sqb"), [P, 512], F32) as sqb, \
         nc.sbuf_tensor(g.uid("kscr"), [P, 1024], F32) as kscr, \
         nc.sbuf_tensor(g.uid("kva"), [P, (NH // 2) * 704], F32) as kv_acc:
        def _(gp):
            scope_fence(g, gp)
            gp.dma_start(out=wk_sb[:, :], in_=dr["wk_r"][0, :, :]).then_inc(g.qG.h, 16)
            gp.dma_start(out=wv_sb_x[:, :], in_=dr["wv_r"][0, :, :]).then_inc(g.qG.h, 16)
            g.qG.inc(2)
            for dc in range(1, DC):
                gp.dma_start(out=wk_sb2[:, (dc - 1) * D:dc * D],
                             in_=dr["wk_r"][dc, :, :]).then_inc(g.qG.h, 16)
                gp.dma_start(out=wv_sb2[:, (dc - 1) * D:dc * D],
                             in_=dr["wv_r"][dc, :, :]).then_inc(g.qG.h, 16)
                g.qG.inc(2)
            gp.memset(tkbuf[:, 0:D], 1.0)
            gp.memset(kv_acc[:], 0.0).then_inc(g.sGP.h, 1)
            g.sGP.inc()
        block.gpsimd(_)
        qg_wkv = g.qG.v
        gp_init = g.sGP.v

        def wk_t(dc):
            return wk_sb[:, :] if dc == 0 else wk_sb2[:, (dc - 1) * D:dc * D]

        def wv_t(dc):
            return wv_sb_x[:, :] if dc == 0 else wv_sb2[:, (dc - 1) * D:dc * D]

        ps_kp = (ps_k0, ps_k1)
        ps_kv = (ps_kva, ps_kvb)
        phaseA_sc = g.sSC.v
        phaseA_dv = g.sDV.v
        sc_kv_ev = {}
        dv_clip = {}
        dv_kv_add = {}
        pe_kv = {}
        pe_tkread = {}

        for tb in range(NTB):
            tpos = tb * P
            for oh in range(2):
                n = tb * 4 + oh
                def _(t, tb=tb, oh=oh, n=n, tpos=tpos):
                    if n == 0:
                        t.wait_ge(g.qG.h, qg_wkv)
                        t.wait_ge(g.sGP.h, gp_init)
                        t.wait_ge(g.sSC.h, phaseA_sc)
                        t.wait_ge(g.sDV.h, phaseA_dv)
                    if n >= 2:
                        t.wait_ge(g.sSC.h, sc_kv_ev[n - 2])
                    pk = ps_kp[n % 2]
                    for dc in range(DC):
                        mm = t.matmul(pk[:, :],
                                      g.XT[:, dc * T + tpos:dc * T + tpos + P],
                                      wk_t(dc)[:, oh * 512:(oh + 1) * 512],
                                      start=(dc == 0), stop=(dc == DC - 1))
                    mm.then_inc(g.sPE.h, 1)
                    g.sPE.inc()
                block.tensor(_)
                pe_k = g.sPE.v

                def _(scalar, oh=oh, n=n, tb=tb):
                    scalar.wait_ge(g.sPE.h, pe_k)
                    if n >= 2 and (n - 2) in dv_clip:
                        scalar.wait_ge(g.sDV.h, dv_clip[n - 2])
                    if SKIP_KEVAC:
                        scalar.nop().then_inc(g.sSC.h, 1)
                    else:
                        scalar.mul(kscr[:, (n % 2) * 512:(n % 2 + 1) * 512],
                                   ps_kp[n % 2][:, :], 0.125).then_inc(g.sSC.h, 1)
                    g.sSC.inc()
                    sc_kv_ev[n] = g.sSC.v
                block.scalar(_)
                sc_k = g.sSC.v

                if not SKIP_CLIP:
                    def _(v, oh=oh, n=n, tb=tb):
                        v.wait_ge(g.sSC.h, sc_k)
                        if tb > 0:
                            v.wait_ge(g.sPE.h, pe_tkread[tb - 1])
                        v.tensor_scalar(ktm[:, oh * 512:(oh + 1) * 512],
                                        kscr[:, (n % 2) * 512:(n % 2 + 1) * 512],
                                        -1.0, 1.0, ALU.max, ALU.min).then_inc(g.sDV.h, 1)
                        g.sDV.inc()
                        dv_clip[n] = g.sDV.v
                    block.vector(_)
            for oh in range(2 if not SKIP_VPROJ else 0):
                n = tb * 4 + 2 + oh
                def _(t, tb=tb, oh=oh, n=n, tpos=tpos):
                    if n >= 2:
                        t.wait_ge(g.sSC.h, sc_kv_ev[n - 2])
                    pk = ps_kp[n % 2]
                    for dc in range(DC):
                        mm = t.matmul(pk[:, :],
                                      g.XT[:, dc * T + tpos:dc * T + tpos + P],
                                      wv_t(dc)[:, oh * 512:(oh + 1) * 512],
                                      start=(dc == 0), stop=(dc == DC - 1))
                    mm.then_inc(g.sPE.h, 1)
                    g.sPE.inc()
                block.tensor(_)
                pe_v = g.sPE.v

                def _(scalar, oh=oh, n=n, tb=tb):
                    scalar.wait_ge(g.sPE.h, pe_v)
                    if tb > 0:
                        scalar.wait_ge(g.sPE.h, pe_tkread[tb - 1])
                    scalar.copy(vtm[:, oh * 512:(oh + 1) * 512],
                                ps_kp[n % 2][:, :]).then_inc(g.sSC.h, 1)
                    g.sSC.inc()
                    sc_kv_ev[n] = g.sSC.v
                block.scalar(_)
            dv_ktm = g.sDV.v
            sc_vtm = g.sSC.v

            if TRUNC == "B1":
                pe_tkread[tb] = g.sPE.v
                continue
            seqops = []
            for fh in range(2):
                seqops += _cheb_seq(ktm[:, fh * 512:(fh + 1) * 512], tkbuf, 512,
                                    w2t, w3t, w4t, w5t, sqa, sqb,
                                    stride=D, col0=fh * 512)
            for k, (eng, fn) in enumerate(seqops):
                if eng == "S":
                    prev_dv = g.sDV.v

                    def _(scalar, fn=fn, prev_dv=prev_dv, k=k):
                        scalar.wait_ge(g.sDV.h, prev_dv)
                        if k == 0:
                            scalar.wait_ge(g.sDV.h, dv_ktm)
                        fn(scalar)
                        scalar.drain().then_inc(g.sSC.h, 1)
                        g.sSC.inc()
                    block.scalar(_)
                else:
                    prev_sc = g.sSC.v

                    def _(v, fn=fn, prev_sc=prev_sc, tb=tb):
                        v.wait_ge(g.sSC.h, prev_sc)
                        if tb > 0:
                            v.wait_ge(g.sPE.h, pe_tkread[tb - 1])
                        fn(v)
                        v.drain().then_inc(g.sDV.h, 1)
                        g.sDV.inc()
                    block.vector(_)
            sc_ft = g.sSC.v
            dv_ft = g.sDV.v

            if TRUNC == "B2":
                pe_tkread[tb] = g.sPE.v
                continue
            for hp in range(NH // 2):
                n = tb * (NH // 2) + hp
                def _(t, tb=tb, hp=hp, n=n):
                    if n == 0:
                        t.wait_ge(g.sGP.h, gp_init)
                    t.wait_ge(g.sSC.h, sc_ft)
                    t.wait_ge(g.sDV.h, dv_ft)
                    t.wait_ge(g.sSC.h, sc_vtm)
                    if n >= 2:
                        t.wait_ge(g.sDV.h, dv_kv_add[n - 2])
                    pk = ps_kv[n % 2]
                    tk3 = tkbuf[:].rearrange("p (m x) -> p m x", x=D)
                    for hh in range(2):
                        h = 2 * hp + hh
                        lhs = vtm[:, h * DH:(h + 1) * DH]
                        mm = t.matmul(pk[hh * 64:hh * 64 + 64, 0:512],
                                      lhs, tk3[:, 0:8, h * DH:(h + 1) * DH],
                                      start=True, stop=True,
                                      tile_position=(0, hh * 64))
                        mm = t.matmul(pk[hh * 64:hh * 64 + 64, 512:704],
                                      lhs, tk3[:, 8:11, h * DH:(h + 1) * DH],
                                      start=True, stop=True,
                                      tile_position=(0, hh * 64))
                    mm.then_inc(g.sPE.h, 1)
                    g.sPE.inc()
                    pe_kv[n] = g.sPE.v
                block.tensor(_)
                pe_n = g.sPE.v

                def _(v, tb=tb, hp=hp, n=n):
                    v.wait_ge(g.sPE.h, pe_n)
                    v.tensor_tensor(kv_acc[:, hp * 704:(hp + 1) * 704],
                                    kv_acc[:, hp * 704:(hp + 1) * 704],
                                    ps_kv[n % 2][:, 0:704], ALU.add).then_inc(g.sDV.h, 1)
                    g.sDV.inc()
                    dv_kv_add[n] = g.sDV.v
                block.vector(_)
            pe_tkread[tb] = pe_kv[tb * (NH // 2) + NH // 2 - 1]

            def _(v):
                v.drain()
            block.vector(_)
        dv_kv_done = g.sDV.v
        if DEBUG_STAGE == "l0a_vt":
            def _(gp):
                gp.wait_ge(g.sDV.h, g.sDV.v)
                gp.wait_ge(g.sSC.h, g.sSC.v)
                gp.wait_ge(g.sPE.h, g.sPE.v)
                gp.wait_ge(g.qG.h, g.qG.v)
                gp.wait_ge(g.qS.h, g.qS.v)
                gp.wait_ge(g.qA.h, g.qA.v)
                gp.dma_start(out=g.out_fm[0:P, 0:1024], in_=vtm[:]).then_inc(g.qG.h, 16)
                gp.dma_start(out=g.out_fm[P:2 * P, 0:1024], in_=ktm[:]).then_inc(g.qG.h, 16)
                for m in range(6):
                    gp.dma_start(out=g.out_fm[(2 + m) * P:(3 + m) * P, 0:1024],
                                 in_=tkbuf[:, m * D:(m + 1) * D]).then_inc(g.qG.h, 16)
                g.qG.inc(8)
            block.gpsimd(_)
            return
        if TRUNC in ("B1", "B2", "B3"):
            return

        def _(gp):
            gp.wait_ge(g.sDV.h, dv_kv_done)
            for hp in range(NH // 2):
                gp.dma_start(out=g.kv_cin[hp * P:(hp + 1) * P, :],
                             in_=kv_acc[:, hp * 704:(hp + 1) * 704]).then_inc(g.qG.h, 16)
                g.qG.inc()
            gp.wait_ge(g.qG.h, g.qG.v)
            gp.collective_compute(
                "AllReduce", ALU.add,
                replica_groups=[[0, 1, 2, 3], [4, 5, 6, 7]],
                ins=[g.kv_cin[:, :]],
                outs=[g.kv_cout[:, :]],
            ).then_inc(g.sCC.h, 1)
            g.sCC.inc()
        block.gpsimd(_)
        cc_kv = g.sCC.v
        if DEBUG_STAGE == "l0a_kv":
            def _(gp):
                gp.wait_ge(g.sCC.h, cc_kv)
                gp.dma_start(out=g.out_fm[0:1024, 0:704],
                             in_=g.kv_cout[:, :]).then_inc(g.qG.h, 16)
                g.qG.inc()
            block.gpsimd(_)
            return

    if TRUNC == "B":
        return
    with nc.psum_tensor(g.uid("ps_t"), [P, 512], F32) as ps_t, \
         nc.psum_tensor(g.uid("ps_o0"), [P, 512], F32) as ps_o0, \
         nc.psum_tensor(g.uid("ps_o1"), [P, 512], F32) as ps_o1, \
         nc.sbuf_tensor(g.uid("kvr"), [P, 704], F32) as kvr, \
         nc.sbuf_tensor(g.uid("bsb"), [P, 704], F32) as bsb, \
         nc.sbuf_tensor(g.uid("kvf"), [P, 704], F32) as kvf, \
         nc.sbuf_tensor(g.uid("kvT"), [P, M0 * 64], BF16) as kvT, \
         nc.sbuf_tensor(g.uid("ftq"), [P, M0 * 1024], BF16) as ftq, \
         nc.sbuf_tensor(g.uid("qh"), [P, 1024], F32) as qh, \
         nc.sbuf_tensor(g.uid("w2q"), [P, 1024], F32) as w2q, \
         nc.sbuf_tensor(g.uid("w3q"), [P, 1024], F32) as w3q, \
         nc.sbuf_tensor(g.uid("w4q"), [P, 1024], F32) as w4q, \
         nc.sbuf_tensor(g.uid("w5q"), [P, 1024], F32) as w5q, \
         nc.sbuf_tensor(g.uid("sqc"), [P, 1024], F32) as sqc, \
         nc.sbuf_tensor(g.uid("sqd"), [P, 1024], F32) as sqd, \
         nc.sbuf_tensor(g.uid("zrow"), [1, P], BF16) as zrow:
        def _(gp):
            scope_fence(g, gp)
            gp.memset(zrow[:], 0.0)
            gp.memset(ftq[:, 0:1024], 1.0).then_inc(g.sGP.h, 1)
            g.sGP.inc()
        block.gpsimd(_)
        gp_z = g.sGP.v

        ps_o = (ps_o0, ps_o1)
        for hp in range(NH // 2):
            def _(gp, hp=hp):
                gp.wait_ge(g.sCC.h, cc_kv)
                if hp > 0:
                    gp.wait_ge(g.sDV.h, g.sDV.v)
                gp.dma_start(out=kvr[:], in_=g.kv_cout[hp * P:(hp + 1) * P, :]).then_inc(g.qG.h, 16)
                gp.dma_start(out=bsb[:], in_=dr["beta_bc"][hp, :, :]).then_inc(g.qG.h, 16)
                g.qG.inc(2)
            block.gpsimd(_)
            qg_kv = g.qG.v

            def _(v, hp=hp):
                v.wait_ge(g.qG.h, qg_kv)
                if hp > 0:
                    v.wait_ge(g.sPE.h, g.sPE.v)
                v.tensor_tensor(kvf[:], kvr[:], bsb[:], ALU.mult).then_inc(g.sDV.h, 1)
                g.sDV.inc()
            block.vector(_)
            dv_kvf = g.sDV.v

            for rnd in range(3):
                m0r = rnd * 4
                nm = min(4, M0 - m0r)
                dv_prev_ev = g.sDV.v

                def _(t, hp=hp, m0r=m0r, nm=nm, rnd=rnd, dv_prev_ev=dv_prev_ev):
                    t.wait_ge(g.sDV.h, dv_kvf if rnd == 0 else dv_prev_ev)
                    for k in range(nm):
                        m = m0r + k
                        t.matmul(ps_t[0:64, k * P:(k + 1) * P],
                                 kvf[:, m * 64:(m + 1) * 64], g.ident_fs[:, :],
                                 start=True, stop=True)
                        mm = t.matmul(ps_t[64:128, k * P:(k + 1) * P],
                                      kvf[:, m * 64:(m + 1) * 64], g.ident_fs[:, :],
                                      start=True, stop=True,
                                      tile_position=(0, 64))
                    mm.then_inc(g.sPE.h, 1)
                    g.sPE.inc()
                block.tensor(_)
                pe_tr = g.sPE.v

                def _(v, m0r=m0r, nm=nm):
                    v.wait_ge(g.sPE.h, pe_tr)
                    srcA = ps_t[0:64, 0:nm * P].rearrange("p (k x) -> p k x", x=P)[:, :, 0:64]
                    srcB = ps_t[64:128, 0:nm * P].rearrange("p (k x) -> p k x", x=P)[:, :, 64:128]
                    v.tensor_copy(kvT[0:64, m0r * 64:(m0r + nm) * 64], srcA)
                    v.drain()
                    v.tensor_copy(kvT[64:128, m0r * 64:(m0r + nm) * 64],
                                  srcB).then_inc(g.sDV.h, 1)
                    g.sDV.inc()
                block.vector(_)
            dv_kvT = g.sDV.v

            dv_oev = {}
            for half in range(2):
                def _(gp, hp=hp, half=half):
                    gp.wait_ge(g.sPE.h, g.sPE.v)
                    gp.wait_ge(g.sDV.h, g.sDV.v)
                    gp.dma_start(out=qh[:],
                                 in_=g.qs_dram[hp * P:(hp + 1) * P,
                                               half * 1024:(half + 1) * 1024]).then_inc(g.qG.h, 16)
                    g.qG.inc()
                block.gpsimd(_)
                qg_qh = g.qG.v

                def _(v):
                    v.wait_ge(g.qG.h, qg_qh)
                    v.engine_nop().then_inc(g.sDV.h, 1)
                    g.sDV.inc()
                block.vector(_)

                seqops = _cheb_seq(qh, ftq, 1024, w2q, w3q, w4q, w5q, sqc, sqd)
                for k, (eng, fn) in enumerate(seqops):
                    if eng == "S":
                        prev_dv = g.sDV.v

                        def _(scalar, fn=fn, prev_dv=prev_dv, k=k):
                            scalar.wait_ge(g.sDV.h, prev_dv)
                            if k == 0:
                                scalar.wait_ge(g.qG.h, qg_qh)
                                scalar.wait_ge(g.sGP.h, gp_z)
                            fn(scalar)
                            scalar.drain().then_inc(g.sSC.h, 1)
                            g.sSC.inc()
                        block.scalar(_)
                    else:
                        prev_sc = g.sSC.v

                        def _(v, fn=fn, prev_sc=prev_sc):
                            v.wait_ge(g.sSC.h, prev_sc)
                            v.wait_ge(g.sPE.h, g.sPE.v)
                            fn(v)
                            v.drain().then_inc(g.sDV.h, 1)
                            g.sDV.inc()
                        block.vector(_)
                sc_ftq = g.sSC.v
                dv_ftq = g.sDV.v

                for tc2 in range(2):
                    n2 = half * 2 + tc2

                    def _(t, hp=hp, half=half, tc2=tc2, n2=n2):
                        t.wait_ge(g.sGP.h, gp_z)
                        t.wait_ge(g.sSC.h, sc_ftq)
                        t.wait_ge(g.sDV.h, dv_ftq)
                        if n2 >= 2:
                            t.wait_ge(g.sDV.h, dv_oev[n2 - 2])
                        po = ps_o[n2 % 2]
                        t.matmul(po[:, :], zrow[0:1, :], g.onesrow_b[0:1, :],
                                 start=True, stop=False)
                        for m in range(M0):
                            for hh in range(2):
                                mm = t.matmul(
                                    po[hh * 64:hh * 64 + 64, :],
                                    kvT[hh * 64:hh * 64 + 64, m * 64:(m + 1) * 64],
                                    ftq[hh * 64:hh * 64 + 64,
                                        m * 1024 + tc2 * 512:m * 1024 + tc2 * 512 + 512],
                                    start=False,
                                    stop=(m == M0 - 1 and hh == 1),
                                    tile_position=(hh * 64, hh * 64))
                        mm.then_inc(g.sPE.h, 1)
                        g.sPE.inc()
                    block.tensor(_)
                    pe_o = g.sPE.v

                    def _(v, hp=hp, half=half, tc2=tc2, n2=n2):
                        v.wait_ge(g.sPE.h, pe_o)
                        t0 = half * 1024 + tc2 * 512
                        v.tensor_copy(g.XT[:, hp * T + t0:hp * T + t0 + 512],
                                      ps_o[n2 % 2][:, :]).then_inc(g.sDV.h, 1)
                        g.sDV.inc()
                        dv_oev[n2] = g.sDV.v
                    block.vector(_)
    dv_attnout = g.sDV.v

    if TRUNC == "D":
        return
    with nc.psum_tensor(g.uid("ps_w0"), [P, 512], F32) as ps_w0, \
         nc.psum_tensor(g.uid("ps_w1"), [P, 512], F32) as ps_w1, \
         nc.sbuf_tensor(g.uid("wo"), [P, DC * D], BF16) as wo_sb:
        def _(gp):
            scope_fence(g, gp)
            for oc in range(DC):
                gp.dma_start(out=wo_sb[:, oc * D:(oc + 1) * D],
                             in_=dr["wo_t"][oc, :, :]).then_inc(g.qG.h, 16)
                g.qG.inc()
        block.gpsimd(_)
        qg_wo = g.qG.v
        ps_w = (ps_w0, ps_w1)
        dv_xev = {}
        for oc in range(DC):
            for tc in range(4):
                n = oc * 4 + tc
                t0 = tc * 512

                def _(t, oc=oc, t0=t0, n=n):
                    if n == 0:
                        t.wait_ge(g.qG.h, qg_wo)
                        t.wait_ge(g.sDV.h, dv_attnout)
                    if n >= 2:
                        t.wait_ge(g.sDV.h, dv_xev[n - 2])
                    pw = ps_w[n % 2]
                    for dc in range(DC):
                        mm = t.matmul(pw[:, :],
                                      wo_sb[:, oc * D + dc * P:oc * D + (dc + 1) * P],
                                      g.XT[:, dc * T + t0:dc * T + t0 + 512],
                                      start=(dc == 0), stop=(dc == DC - 1))
                    mm.then_inc(g.sPE.h, 1)
                    g.sPE.inc()
                block.tensor(_)
                pe_w = g.sPE.v

                def _(v, oc=oc, t0=t0, n=n):
                    v.wait_ge(g.sPE.h, pe_w)
                    v.tensor_tensor(g.X[:, oc * T + t0:oc * T + t0 + 512],
                                    g.X[:, oc * T + t0:oc * T + t0 + 512],
                                    ps_w[n % 2][:, :], ALU.add).then_inc(g.sDV.h, 1)
                    g.sDV.inc()
                    dv_xev[n] = g.sDV.v
                block.vector(_)


def _cheb_seq(src_f32, ftbuf, width, w2, w3, w4, w5, sqa, sqb, stride=None, col0=0):
    SQ2 = 1.4142135623730951
    if stride is None:
        stride = width

    def ft(m):
        return ftbuf[:, m * stride + col0:m * stride + col0 + width]

    ops = []
    ops.append(("S", lambda s: s.copy(ft(1), src_f32[:])))
    ops.append(("S", lambda s: s.activation(sqa[:], src_f32[:], AF.Square, scale=SQ2)))
    ops.append(("V", lambda v: v.tensor_scalar(w2[:], sqa[:], -1.0, None, ALU.add)))
    ops.append(("S", lambda s: s.copy(ft(2), w2[:])))
    ops.append(("V", lambda v: (v.scalar_tensor_tensor(sqb[:], src_f32[:], 2.0, w2[:],
                                                       ALU.mult, ALU.mult),
                                v.drain(),
                                v.tensor_tensor(w3[:], sqb[:], src_f32[:], ALU.subtract))))
    ops.append(("S", lambda s: s.copy(ft(3), w3[:])))
    ops.append(("S", lambda s: s.activation(sqa[:], w2[:], AF.Square, scale=SQ2)))
    ops.append(("V", lambda v: v.tensor_scalar(w4[:], sqa[:], -1.0, None, ALU.add)))
    ops.append(("S", lambda s: s.copy(ft(4), w4[:])))
    ops.append(("V", lambda v: (v.scalar_tensor_tensor(sqb[:], w2[:], 2.0, w3[:],
                                                       ALU.mult, ALU.mult),
                                v.drain(),
                                v.tensor_tensor(w5[:], sqb[:], src_f32[:], ALU.subtract))))
    ops.append(("S", lambda s: s.copy(ft(5), w5[:])))
    ops.append(("S", lambda s: s.activation(sqa[:], w3[:], AF.Square, scale=SQ2)))
    ops.append(("V", lambda v: v.tensor_scalar(ft(6), sqa[:], -1.0, None, ALU.add)))
    ops.append(("V", lambda v: (v.scalar_tensor_tensor(sqb[:], w3[:], 2.0, w4[:],
                                                       ALU.mult, ALU.mult),
                                v.drain(),
                                v.tensor_tensor(ft(7), sqb[:], src_f32[:], ALU.subtract))))
    ops.append(("S", lambda s: s.activation(sqa[:], w4[:], AF.Square, scale=SQ2)))
    ops.append(("V", lambda v: v.tensor_scalar(ft(8), sqa[:], -1.0, None, ALU.add)))
    ops.append(("V", lambda v: (v.scalar_tensor_tensor(sqb[:], w4[:], 2.0, w5[:],
                                                       ALU.mult, ALU.mult),
                                v.drain(),
                                v.tensor_tensor(ft(9), sqb[:], src_f32[:], ALU.subtract))))
    ops.append(("S", lambda s: s.activation(sqa[:], w5[:], AF.Square, scale=SQ2)))
    ops.append(("V", lambda v: v.tensor_scalar(ft(10), sqa[:], -1.0, None, ALU.add)))
    return ops


def emit_l1ffn(g):
    nc, block = g.nc, g.block
    dr = g.dr
    ln_dv = g.ln_done_dv

    es2 = ExitStack()
    wg_sb = es2.enter_context(nc.sbuf_tensor(g.uid("wg"), [P, DC * E1], F32))
    cw_sb = es2.enter_context(nc.sbuf_tensor(g.uid("cw"), [1, E1], F32))
    iota_bc = es2.enter_context(nc.sbuf_tensor(g.uid("iot"), [P, E1], F32))
    O_str = es2.enter_context(nc.sbuf_tensor(g.uid("Ostr"), [P, 16 * E1], F32))
    idx_str = es2.enter_context(nc.sbuf_tensor(g.uid("idxs"), [P, 16], F32))
    gate_str = es2.enter_context(nc.sbuf_tensor(g.uid("gats"), [P, 16], F32))
    loc_str = es2.enter_context(nc.sbuf_tensor(g.uid("locs"), [P, 16], F32))
    csc_str = es2.enter_context(nc.sbuf_tensor(g.uid("csc"), [P, 16], F32))
    gidx_u = es2.enter_context(nc.sbuf_tensor(g.uid("gidx"), [P, 16], U32))
    sgx_f = es2.enter_context(nc.sbuf_tensor(g.uid("sgxf"), [P, 16], F32))
    cnt8_sb = es2.enter_context(nc.sbuf_tensor(g.uid("cnt8"), [8, E1], F32))
    cmsk_sb = es2.enter_context(nc.sbuf_tensor(g.uid("cmsk"), [8, 1], F32))
    rot_sb = es2.enter_context(nc.sbuf_tensor(g.uid("rot"), [P, 1], F32))
    offs_sb = es2.enter_context(nc.sbuf_tensor(g.uid("offs"), [1, E1], F32))
    cntc_sb = es2.enter_context(nc.sbuf_tensor(g.uid("cntc"), [E1, 1], F32))
    cntr_sb = es2.enter_context(nc.sbuf_tensor(g.uid("cntr"), [1, E1], F32))
    tiny = es2.enter_context(nc.sbuf_tensor(g.uid("tiny"), [P, 128], F32))
    stage = es2.enter_context(nc.sbuf_tensor(g.uid("stg"), [P, 2 * D], BF16))
    gxs_f_sb = es2.enter_context(nc.sbuf_tensor(g.uid("gxf"), [P, 16], F32))
    gxs_rot = es2.enter_context(nc.sbuf_tensor(g.uid("gxr"), [P, 16], U32))
    sidx_sb = es2.enter_context(nc.sbuf_tensor(g.uid("sidx"), [P, 2], U32))
    tokid_sb = es2.enter_context(nc.sbuf_tensor(g.uid("tokid"), [P, 32], U32))
    with es2:
        def _(gp):
            scope_fence(g, gp)
            gp.dma_start(out=wg_sb[:], in_=dr["wg_t"][:, :]).then_inc(g.qG.h, 16)
            gp.dma_start(out=cmsk_sb[:], in_=dr["coremask"][:, :]).then_inc(g.qG.h, 16)
            gp.dma_start(out=rot_sb[:], in_=dr["rotvec"][:, :]).then_inc(g.qG.h, 16)
            g.qG.inc(3)
            gp.iota(iota_bc[:], pattern=[[1, E1]], base=0, channel_multiplier=0,
                    allow_small_or_imprecise_dtypes=True)
            gp.engine_nop().then_inc(g.sGP.h, 1)
            g.sGP.inc()
        block.gpsimd(_)

        qg_init = g.qG.v
        gp_init = g.sGP.v

        with nc.psum_tensor(g.uid("ps_tr"), [P, 1024], BF16) as ps_tr:
            qa_st = {}
            dv_st = {}
            for tb in range(16):
                def _(t, tb=tb):
                    if tb == 0:
                        t.wait_ge(g.sDV.h, ln_dv)
                        t.wait_ge(g.sSC.h, g.sSC.v)
                    else:
                        t.wait_ge(g.sDV.h, dv_st[tb - 1])
                    for dc in range(DC):
                        mm = t.transpose(ps_tr[:, dc * P:(dc + 1) * P],
                                         g.XT[:, dc * T + tb * P:dc * T + (tb + 1) * P],
                                         g.ident_bs[:, :])
                    mm.then_inc(g.sPE.h, 1)
                    g.sPE.inc()
                block.tensor(_)
                pe_tr = g.sPE.v

                def _(v, tb=tb, pe_tr=pe_tr):
                    v.wait_ge(g.sPE.h, pe_tr)
                    if tb >= 2:
                        v.wait_ge(g.qA.h, qa_st[tb - 2])
                    v.tensor_copy(stage[:, (tb % 2) * D:(tb % 2 + 1) * D],
                                  ps_tr[:, :]).then_inc(g.sDV.h, 1)
                    g.sDV.inc()
                    dv_st[tb] = g.sDV.v
                block.vector(_)

                def _(scalar, tb=tb):
                    scalar.wait_ge(g.sDV.h, dv_st[tb])
                    scalar.dma_start(out=g.ag_in_tm[tb * P:(tb + 1) * P, :],
                                     in_=stage[:, (tb % 2) * D:(tb % 2 + 1) * D]).then_inc(g.qA.h, 16)
                    g.qA.inc()
                    qa_st[tb] = g.qA.v
                block.scalar(_)
            qa_tm = g.qA.v

        def _(gp):
            gp.wait_ge(g.qA.h, qa_tm)
            gp.collective_compute(
                "AllGather", ALU.bypass, replica_groups=[list(range(NCORES))],
                ins=[g.ag_in_tm[:, :]], outs=[g.ag_out_tm8[:, :]],
            ).then_inc(g.sCC.h, 1)
            g.sCC.inc()
        block.gpsimd(_)
        cc_tm = g.sCC.v
        if TRUNC == "F0":
            def _(gp):
                gp.wait_ge(g.sCC.h, cc_tm)
            block.gpsimd(_)
            return

        with nc.psum_tensor(g.uid("ps_ra"), [P, 512], F32) as ps_ra, \
             nc.psum_tensor(g.uid("ps_rb"), [P, 512], F32) as ps_rb, \
             nc.psum_tensor(g.uid("ps_cnt"), [P, 512], F32) as ps_cnt, \
             nc.sbuf_tensor(g.uid("lnab"), [1, 2 * T], F32) as lnab_sb:
            ps_r = (ps_ra, ps_rb)

            def _(sync):
                sync.wait_ge(g.qA.h, g.qA.v)
                sync.dma_start(out=lnab_sb[0:1, 0:T],
                               in_=g.lnab_dram[0:1, :]).then_inc(g.qS.h, 16)
                sync.dma_start(out=lnab_sb[0:1, T:2 * T],
                               in_=g.lnab_dram[1:2, :]).then_inc(g.qS.h, 16)
                g.qS.inc(2)
            block.sync(_)
            qs_init = g.qS.v

            def _(t):
                t.wait_ge(g.qG.h, qg_init)
                t.wait_ge(g.sGP.h, gp_init)
                t.wait_ge(g.qS.h, qs_init)
                for dc in range(DC):
                    mm = t.matmul(ps_cnt[0:1, 0:E1], g.onescol_f[:],
                                  wg_sb[:, dc * E1:(dc + 1) * E1],
                                  start=(dc == 0), stop=(dc == DC - 1))
                mm.then_inc(g.sPE.h, 1)
                g.sPE.inc()
            block.tensor(_)
            pe_cw = g.sPE.v

            def _(v, pe_cw=pe_cw):
                v.wait_ge(g.sPE.h, pe_cw)
                v.tensor_copy(cw_sb[:], ps_cnt[0:1, 0:E1]).then_inc(g.sDV.h, 1)
                g.sDV.inc()
            block.vector(_)
            dv_cw = g.sDV.v

            lgt = tiny[:, 0:E1]
            m1 = tiny[:, 32:33]
            negm1 = tiny[:, 33:34]
            acol = tiny[:, 34:35]
            bcw = tiny[:, 36:36 + E1]
            Zc = tiny[:, 52:53]
            scr = tiny[:, 64:64 + E1]

            dv_rt = {}
            for tb in range(16):
                def _(t, tb=tb):
                    if tb == 0:
                        t.wait_ge(g.sDV.h, dv_cw)
                    if tb >= 2:
                        t.wait_ge(g.sDV.h, dv_rt[tb - 2])
                    pr = ps_r[tb % 2]
                    for dc in range(DC):
                        t.matmul(pr[:, 0:E1],
                                 g.X[:, dc * T + tb * P:dc * T + (tb + 1) * P],
                                 wg_sb[:, dc * E1:(dc + 1) * E1],
                                 start=(dc == 0), stop=(dc == DC - 1))
                    t.matmul(pr[:, E1:2 * E1], lnab_sb[0:1, T + tb * P:T + (tb + 1) * P],
                             cw_sb[:], start=True, stop=True)
                    mm = t.transpose(pr[:, 2 * E1:2 * E1 + 1],
                                     lnab_sb[0:1, tb * P:(tb + 1) * P],
                                     g.ident_fs[0:1, 0:1])
                    mm.then_inc(g.sPE.h, 1)
                    g.sPE.inc()
                block.tensor(_)
                pe_r = g.sPE.v

                def _(v, tb=tb, pe_r=pe_r):
                    v.wait_ge(g.sPE.h, pe_r)
                    v.tensor_copy(acol, ps_r[tb % 2][:, 2 * E1:2 * E1 + 1])
                    v.tensor_copy(bcw, ps_r[tb % 2][:, E1:2 * E1])
                    v.drain()
                    v.scalar_tensor_tensor(lgt, ps_r[tb % 2][:, 0:E1], acol, bcw,
                                           ALU.mult, ALU.add)
                    v.drain()
                    v.tensor_reduce(m1, lgt, axis=AX.X, op=ALU.max)
                    v.drain()
                    v.tensor_scalar(O_str[:, tb * E1:(tb + 1) * E1], lgt, m1, None,
                                    ALU.is_ge)
                    v.tensor_scalar(negm1, m1, -1.0, None, ALU.mult)
                    v.drain()
                    v.tensor_tensor(scr, O_str[:, tb * E1:(tb + 1) * E1],
                                    iota_bc[:], ALU.mult)
                    v.drain()
                    v.tensor_reduce(idx_str[:, tb:tb + 1], scr, axis=AX.X,
                                    op=ALU.add).then_inc(g.sDV.h, 1)
                    g.sDV.inc()
                block.vector(_)
                dv_l = g.sDV.v

                def _(scalar, tb=tb, dv_l=dv_l):
                    scalar.wait_ge(g.sDV.h, dv_l)
                    scalar.activation(scr, lgt, AF.Exp, bias=negm1,
                                      accum_out=Zc).then_inc(g.sSC.h, 1)
                    g.sSC.inc()
                block.scalar(_)
                sc_e = g.sSC.v

                def _(v, tb=tb, sc_e=sc_e):
                    v.wait_ge(g.sSC.h, sc_e)
                    v.reciprocal(gate_str[:, tb:tb + 1], Zc).then_inc(g.sDV.h, 1)
                    g.sDV.inc()
                    dv_rt[tb] = g.sDV.v
                block.vector(_)
            dv_route = g.sDV.v

            def _(t):
                t.wait_ge(g.sDV.h, dv_route)
                for tb in range(16):
                    mm = t.matmul(ps_cnt[0:E1, 64:65], O_str[:, tb * E1:(tb + 1) * E1],
                                  g.onescol_f[:], start=(tb == 0), stop=(tb == 15))
                mm.then_inc(g.sPE.h, 1)
                g.sPE.inc()
            block.tensor(_)
            pe_cnt = g.sPE.v

            def _(v, pe_cnt=pe_cnt):
                v.wait_ge(g.sPE.h, pe_cnt)
                v.tensor_copy(cntc_sb[:], ps_cnt[0:E1, 64:65]).then_inc(g.sDV.h, 1)
                g.sDV.inc()
            block.vector(_)
            dv_cc = g.sDV.v

            def _(t, dv_cc=dv_cc):
                t.wait_ge(g.sDV.h, dv_cc)
                t.transpose(ps_cnt[0:1, 128:128 + E1], cntc_sb[:],
                            g.ident_fs[0:E1, 0:E1]).then_inc(g.sPE.h, 1)
                g.sPE.inc()
            block.tensor(_)
            pe_cr = g.sPE.v

            def _(v, pe_cr=pe_cr):
                v.wait_ge(g.sPE.h, pe_cr)
                v.tensor_copy(cntr_sb[:], ps_cnt[0:1, 128:128 + E1]).then_inc(g.sDV.h, 1)
                g.sDV.inc()
            block.vector(_)
            dv_cr = g.sDV.v

            def _(gp, dv_cr=dv_cr):
                gp.wait_ge(g.sDV.h, dv_cr)
                gp.dma_start(out=g.rt_in[:, :], in_=cntr_sb[:]).then_inc(g.qG.h, 16)
                g.qG.inc()
                gp.wait_ge(g.qG.h, g.qG.v)
                gp.collective_compute(
                    "AllGather", ALU.bypass, replica_groups=[list(range(NCORES))],
                    ins=[g.rt_in[:, :]], outs=[g.rt_out[:, :]],
                ).then_inc(g.sCC.h, 1)
                g.sCC.inc()
                gp.wait_ge(g.sCC.h, g.sCC.v)
                gp.dma_start(out=cnt8_sb[:], in_=g.rt_out[:, :]).then_inc(g.qG.h, 16)
                g.qG.inc()
            block.gpsimd(_)
            qg_cnt8 = g.qG.v

            def _(t, qg_cnt8=qg_cnt8):
                t.wait_ge(g.qG.h, qg_cnt8)
                t.matmul(ps_cnt[0:1, 192:192 + E1], cmsk_sb[:], cnt8_sb[:],
                         start=True, stop=True).then_inc(g.sPE.h, 1)
                g.sPE.inc()
            block.tensor(_)
            pe_pf = g.sPE.v

            def _(v, pe_pf=pe_pf):
                v.wait_ge(g.sPE.h, pe_pf)
                v.tensor_copy(offs_sb[:], ps_cnt[0:1, 192:192 + E1]).then_inc(g.sDV.h, 1)
                g.sDV.inc()
            block.vector(_)
            dv_offs = g.sDV.v

            dv_loc = {}
            for tb in range(16):
                def _(t, tb=tb):
                    if tb == 0:
                        t.wait_ge(g.sDV.h, dv_offs)
                    else:
                        t.wait_ge(g.sDV.h, dv_loc[tb - 1])
                    pr = ps_r[tb % 2]
                    t.matmul(pr[:, 64:64 + E1], g.onesrow_f[:], offs_sb[:],
                             start=True, stop=False)
                    t.matmul(pr[:, 64:64 + E1], g.lt_ss[:, :],
                             O_str[:, tb * E1:(tb + 1) * E1],
                             start=False, stop=True)
                    mm = t.matmul(pr[0:1, 96:96 + E1], g.onescol_f[:],
                                  O_str[:, tb * E1:(tb + 1) * E1],
                                  start=True, stop=True)
                    mm.then_inc(g.sPE.h, 1)
                    g.sPE.inc()
                block.tensor(_)
                pe_lc = g.sPE.v

                def _(v, tb=tb, pe_lc=pe_lc):
                    v.wait_ge(g.sPE.h, pe_lc)
                    pr = ps_r[tb % 2]
                    v.tensor_tensor(scr, pr[:, 64:64 + E1],
                                    O_str[:, tb * E1:(tb + 1) * E1], ALU.mult)
                    v.drain()
                    v.tensor_reduce(loc_str[:, tb:tb + 1], scr, axis=AX.X, op=ALU.add)
                    v.drain()
                    v.tensor_tensor(offs_sb[:], offs_sb[:], pr[0:1, 96:96 + E1], ALU.add)
                    v.tensor_scalar(tiny[:, 100:101], loc_str[:, tb:tb + 1], 1023.0,
                                    None, ALU.min)
                    v.drain()
                    v.scalar_tensor_tensor(tiny[:, 101:102], idx_str[:, tb:tb + 1],
                                           1024.0, tiny[:, 100:101], ALU.mult, ALU.add)
                    v.tensor_scalar(tiny[:, 102:103], loc_str[:, tb:tb + 1], 1024.0,
                                    None, ALU.is_ge)
                    v.drain()
                    v.tensor_copy(gidx_u[:, tb:tb + 1], tiny[:, 101:102])
                    v.scalar_tensor_tensor(sgx_f[:, tb:tb + 1], tiny[:, 102:103], 1e8,
                                           tiny[:, 101:102], ALU.mult, ALU.add)
                    v.tensor_scalar(tiny[:, 104:105], tiny[:, 102:103], -0.5, 0.5,
                                    ALU.mult, ALU.add)
                    v.drain()
                    v.tensor_tensor(csc_str[:, tb:tb + 1], tiny[:, 104:105],
                                    gate_str[:, tb:tb + 1],
                                    ALU.mult).then_inc(g.sDV.h, 1)
                    g.sDV.inc()
                    dv_loc[tb] = g.sDV.v
                block.vector(_)
            dv_locdone = g.sDV.v

        def _(gp, dv_locdone=dv_locdone):
            gp.wait_ge(g.sDV.h, dv_locdone)
            gp.dma_start(out=g.gx_in[:, :], in_=sgx_f[:]).then_inc(g.qG.h, 16)
            g.qG.inc()
            gp.wait_ge(g.qG.h, g.qG.v)
            gp.collective_compute(
                "AllGather", ALU.bypass, replica_groups=[list(range(NCORES))],
                ins=[g.gx_in[:, :]], outs=[g.gx_out[:, :]],
            ).then_inc(g.sCC.h, 1)
            g.sCC.inc()
        block.gpsimd(_)
        cc_gx = g.sCC.v

        dv_rot = {}
        qg_sc = {}
        gp_tok = {}
        for cp in range(NCORES):
            def _(gp, cp=cp):
                if cp == 0:
                    gp.wait_ge(g.sCC.h, cc_gx)
                else:
                    gp.wait_ge(g.sDV.h, dv_rot[cp - 1])
                gp.dma_start(out=gxs_f_sb[:],
                             in_=g.gx_out[cp * P:(cp + 1) * P, :]).then_inc(g.qG.h, 16)
                g.qG.inc()
                gp.iota(tokid_sb[:], pattern=[[P, 16], [0, 2]], base=cp * T, channel_multiplier=1)
                gp.engine_nop().then_inc(g.sGP.h, 1)
                g.sGP.inc()
                gp_tok[cp] = g.sGP.v
            block.gpsimd(_)
            qg_gx = g.qG.v

            def _(v, cp=cp, qg_gx=qg_gx):
                v.wait_ge(g.qG.h, qg_gx)
                if cp > 0:
                    v.wait_ge(g.qG.h, qg_sc[cp - 1])
                v.tensor_scalar(tiny[:, 106:122], gxs_f_sb[:], rot_sb[:, 0:1], None,
                                ALU.add)
                v.drain()
                v.tensor_scalar(tiny[:, 64:80], tiny[:, 106:122], 16384.0, None,
                                ALU.is_ge)
                v.drain()
                v.scalar_tensor_tensor(tiny[:, 106:122], tiny[:, 64:80], -16384.0,
                                       tiny[:, 106:122], ALU.mult, ALU.add)
                v.drain()
                v.tensor_copy(gxs_rot[:], tiny[:, 106:122]).then_inc(g.sDV.h, 1)
                g.sDV.inc()
                dv_rot[cp] = g.sDV.v
            block.vector(_)

            for tbg in range(4):
                def _(gp, cp=cp, tbg=tbg):
                    if tbg == 0:
                        gp.wait_ge(g.sDV.h, dv_rot[cp])
                        gp.wait_ge(g.sGP.h, gp_tok[cp])
                    for tb in range(tbg * 4, (tbg + 1) * 4):
                        gp.indirect_dma_start(
                            out=g.s2t[:, :],
                            out_offset=bass.IndirectOffsetOnAxis(gxs_rot[:, tb:tb + 1], 0),
                            in_=tokid_sb[:, 2 * tb:2 * tb + 2], in_offset=None,
                            bounds_check=g.bndreg, oob_is_err=False,
                        ).then_inc(g.qG.h, 16)
                        g.qG.inc()
                        gp.wait_ge(g.qG.h, g.qG.v)
                    qg_sc[cp] = g.qG.v
                block.gpsimd(_)
        qg_s2t = g.qG.v
        if TRUNC == "F3":
            def _(gp):
                gp.wait_ge(g.qG.h, qg_s2t)
            block.gpsimd(_)
            return

        with nc.sbuf_tensor(g.uid("dtile"), [P, D], BF16) as dtile, \
             nc.sbuf_tensor(g.uid("dfm"), [P, DC * 1024], BF16) as disp_fm, \
             nc.sbuf_tensor(g.uid("hfm"), [P, FC1 * 512], BF16) as h_fm, \
             nc.sbuf_tensor(g.uid("eof"), [P, DC * 512], BF16) as eof, \
             nc.sbuf_tensor(g.uid("w1t"), [P, 2 * D], BF16) as w1t, \
             nc.sbuf_tensor(g.uid("w2t"), [P, 2 * FC1 * P], BF16) as w2t, \
             nc.psum_tensor(g.uid("ps_m0"), [P, 512], F32) as ps_m0, \
             nc.psum_tensor(g.uid("ps_m1"), [P, 512], F32) as ps_m1, \
             nc.psum_tensor(g.uid("ps_dt"), [P, 1024], BF16) as ps_dt:
            ps_m = (ps_m0, ps_m1)
            pe_dtr = {}
            qg_w1 = {}
            sc_w1 = {}
            pe_w1 = {}
            qg_w2 = {}
            pe_w2 = {}
            dv_w2 = {}
            stq = {"i": 0, "hist": {}}
            dv_dfm = {}

            def stage_wait(eng, slot_i):
                if slot_i >= 2 and (slot_i - 2) in stq["hist"]:
                    eng.wait_ge(g.qA.h, stq["hist"][slot_i - 2])

            for le in range(2):
                for sc in range(8):
                    def _(gp, le=le, sc=sc):
                        if le == 0 and sc == 0:
                            gp.wait_ge(g.sCC.h, cc_tm)
                            gp.wait_ge(g.qG.h, qg_s2t)
                            scope_fence(g, gp)
                        else:
                            prev = (le, sc - 1) if sc > 0 else (le - 1, 7)
                            gp.wait_ge(g.sPE.h, pe_dtr[prev])
                        gp.dma_start(out=sidx_sb[:, 0:2],
                                     in_=g.s2t[le * CAP + sc * P:le * CAP + (sc + 1) * P, :]).then_inc(g.qG.h, 16)
                        g.qG.inc()
                        gp.memset(dtile[:], 0.0)
                        gp.wait_ge(g.qG.h, g.qG.v)
                        gp.indirect_dma_start(
                            out=dtile[:, :], out_offset=None,
                            in_=g.ag_out_tm8[:, :],
                            in_offset=bass.IndirectOffsetOnAxis(sidx_sb[:, 0:1], 0),
                            bounds_check=g.bndreg, oob_is_err=False,
                        ).then_inc(g.qG.h, 16)
                        g.qG.inc()
                    block.gpsimd(_)
                    qg_dt = g.qG.v

                    def _(t, le=le, sc=sc, qg_dt=qg_dt, dv_now=None):
                        t.wait_ge(g.qG.h, qg_dt)
                        if (le, sc) != (0, 0):
                            prev = (le, sc - 1) if sc > 0 else (le - 1, 7)
                            t.wait_ge(g.sDV.h, dv_dfm[prev])
                        if sc == 0:
                            t.wait_ge(g.sDV.h, g.sDV.v)
                        for dc in range(DC):
                            mm = t.transpose(ps_dt[:, dc * P:(dc + 1) * P],
                                             dtile[:, dc * P:(dc + 1) * P],
                                             g.ident_bs[:, :])
                        mm.then_inc(g.sPE.h, 1)
                        g.sPE.inc()
                        pe_dtr[(le, sc)] = g.sPE.v
                    block.tensor(_)
                    pe_t = g.sPE.v

                    def _(v, le=le, sc=sc, pe_t=pe_t):
                        v.wait_ge(g.sPE.h, pe_t)
                        for dc in range(DC):
                            mm = v.tensor_copy(
                                disp_fm[:, dc * 1024 + sc * P:dc * 1024 + (sc + 1) * P],
                                ps_dt[:, dc * P:(dc + 1) * P])
                        mm.then_inc(g.sDV.h, 1)
                        g.sDV.inc()
                        dv_dfm[(le, sc)] = g.sDV.v
                    block.vector(_)
                dv_dfm_done = g.sDV.v

                for sh in range(2):
                    s0 = sh * 512
                    for hc in range(FC1):
                        i = (le * 2 + sh) * FC1 + hc
                        par = i % 2

                        def _(gp, le=le, hc=hc, i=i, par=par):
                            if i >= 2:
                                gp.wait_ge(g.sPE.h, pe_w1[i - 2])
                            gp.dma_start(out=w1t[:, par * D:(par + 1) * D],
                                         in_=dr["w1_t"][le, hc, :, :]).then_inc(g.qG.h, 16)
                            g.qG.inc()
                            qg_w1[i] = g.qG.v
                        block.gpsimd(_)

                        def _(t, le=le, sh=sh, hc=hc, i=i, par=par, s0=s0,
                              dv_dfm_done=dv_dfm_done):
                            t.wait_ge(g.qG.h, qg_w1[i])
                            if hc == 0 and sh == 0:
                                t.wait_ge(g.sDV.h, dv_dfm_done)
                            if hc < 2:
                                t.wait_ge(g.sDV.h, g.sDV.v)
                            if i >= 2:
                                t.wait_ge(g.sSC.h, sc_w1[i - 2])
                            pm = ps_m[i % 2]
                            for dc in range(DC):
                                mm = t.matmul(pm[:, :],
                                              w1t[:, par * D + dc * P:par * D + (dc + 1) * P],
                                              disp_fm[:, dc * 1024 + s0:dc * 1024 + s0 + 512],
                                              start=(dc == 0), stop=(dc == DC - 1))
                            mm.then_inc(g.sPE.h, 1)
                            g.sPE.inc()
                            pe_w1[i] = g.sPE.v
                        block.tensor(_)
                        pe_h = g.sPE.v

                        def _(scalar, hc=hc, i=i, pe_h=pe_h):
                            scalar.wait_ge(g.sPE.h, pe_h)
                            scalar.activation(h_fm[:, hc * 512:(hc + 1) * 512],
                                              ps_m[i % 2][:, :],
                                              AF.Relu).then_inc(g.sSC.h, 1)
                            g.sSC.inc()
                            sc_w1[i] = g.sSC.v
                        block.scalar(_)
                    sc_h_done = g.sSC.v

                    for dc in range(DC):
                        j = (le * 2 + sh) * DC + dc
                        par = j % 2

                        def _(gp, le=le, dc=dc, j=j, par=par):
                            if j >= 2:
                                gp.wait_ge(g.sPE.h, pe_w2[j - 2])
                            gp.dma_start(out=w2t[:, par * FC1 * P:(par + 1) * FC1 * P],
                                         in_=dr["w2_t"][le, dc, :, :]).then_inc(g.qG.h, 16)
                            g.qG.inc()
                            qg_w2[j] = g.qG.v
                        block.gpsimd(_)

                        def _(t, le=le, sh=sh, dc=dc, j=j, par=par,
                              sc_h_done=sc_h_done):
                            t.wait_ge(g.qG.h, qg_w2[j])
                            t.wait_ge(g.sSC.h, sc_h_done)
                            if j >= 2:
                                t.wait_ge(g.sDV.h, dv_w2[j - 2])
                            pm = ps_m[j % 2]
                            for hc in range(FC1):
                                mm = t.matmul(pm[:, :],
                                              w2t[:, par * FC1 * P + hc * P:par * FC1 * P + (hc + 1) * P],
                                              h_fm[:, hc * 512:(hc + 1) * 512],
                                              start=(hc == 0), stop=(hc == FC1 - 1))
                            mm.then_inc(g.sPE.h, 1)
                            g.sPE.inc()
                            pe_w2[j] = g.sPE.v
                        block.tensor(_)
                        pe_e = g.sPE.v

                        def _(v, dc=dc, j=j, pe_e=pe_e):
                            v.wait_ge(g.sPE.h, pe_e)
                            v.tensor_copy(eof[:, dc * 512:(dc + 1) * 512],
                                          ps_m[j % 2][:, :]).then_inc(g.sDV.h, 1)
                            g.sDV.inc()
                            dv_w2[j] = g.sDV.v
                        block.vector(_)
                    dv_w2_done = g.sDV.v

                    dv_eot = {}
                    for sc2 in range(4):
                        def _(t, le=le, sh=sh, sc2=sc2, dv_w2_done=dv_w2_done):
                            t.wait_ge(g.sDV.h, dv_w2_done)
                            if sc2 > 0:
                                t.wait_ge(g.sDV.h, dv_eot[sc2 - 1])
                            for dc in range(DC):
                                mm = t.transpose(ps_dt[:, dc * P:(dc + 1) * P],
                                                 eof[:, dc * 512 + sc2 * P:dc * 512 + (sc2 + 1) * P],
                                                 g.ident_bs[:, :])
                            mm.then_inc(g.sPE.h, 1)
                            g.sPE.inc()
                        block.tensor(_)
                        pe_et = g.sPE.v

                        slot_i = stq["i"]
                        stq["i"] += 1

                        def _(v, sc2=sc2, pe_et=pe_et, slot_i=slot_i):
                            v.wait_ge(g.sPE.h, pe_et)
                            stage_wait(v, slot_i)
                            v.tensor_copy(stage[:, (slot_i % 2) * D:(slot_i % 2 + 1) * D],
                                          ps_dt[:, :]).then_inc(g.sDV.h, 1)
                            g.sDV.inc()
                            dv_eot[sc2] = g.sDV.v
                        block.vector(_)

                        def _(scalar, le=le, sh=sh, sc2=sc2, slot_i=slot_i):
                            scalar.wait_ge(g.sDV.h, dv_eot[sc2])
                            r0 = le * CAP + sh * 512 + sc2 * P
                            scalar.dma_start(out=g.eo_in[r0:r0 + P, :],
                                             in_=stage[:, (slot_i % 2) * D:(slot_i % 2 + 1) * D]).then_inc(g.qA.h, 16)
                            g.qA.inc()
                            stq["hist"][slot_i] = g.qA.v
                        block.scalar(_)
            qa_eo_done = g.qA.v

        def _(gp, qa_eo_done=qa_eo_done):
            gp.wait_ge(g.qA.h, qa_eo_done)
            gp.collective_compute(
                "AllGather", ALU.bypass, replica_groups=[list(range(NCORES))],
                ins=[g.eo_in[:, :]], outs=[g.eo_out[:, :]],
            ).then_inc(g.sCC.h, 1)
            g.sCC.inc()
        block.gpsimd(_)
        cc_eo = g.sCC.v
        if TRUNC == "F5":
            def _(gp):
                gp.wait_ge(g.sCC.h, cc_eo)
            block.gpsimd(_)
            return

        with nc.sbuf_tensor(g.uid("ctile"), [P, D], BF16) as ctile, \
             nc.sbuf_tensor(g.uid("ctf"), [P, D], F32) as ctf, \
             nc.psum_tensor(g.uid("ps_ca"), [P, 512], F32) as ps_ca, \
             nc.psum_tensor(g.uid("ps_cb"), [P, 512], F32) as ps_cb:
            ps_c = (ps_ca, ps_cb)
            dv_cmb = {}
            dv_cadd = {}
            pe_tr_hist = {}
            for tb in range(CMB_N):
                def _(gp, tb=tb):
                    if tb == 0:
                        gp.wait_ge(g.sCC.h, cc_eo)
                        scope_fence(g, gp)
                    else:
                        gp.wait_ge(g.sDV.h, dv_cmb[tb - 1])
                    if CMB_MODE == "plaindma":
                        gp.dma_start(out=ctile[:, :],
                                     in_=g.eo_out[tb * P:(tb + 1) * P, :]).then_inc(g.qG.h, 16)
                    else:
                        gp.indirect_dma_start(
                            out=ctile[:, :], out_offset=None,
                            in_=g.eo_out[:, :],
                            in_offset=bass.IndirectOffsetOnAxis(gidx_u[:, tb:tb + 1], 0),
                            bounds_check=g.bndreg, oob_is_err=False,
                        ).then_inc(g.qG.h, 16)
                    g.qG.inc()
                block.gpsimd(_)
                qg_ct = g.qG.v

                def _(v, tb=tb, qg_ct=qg_ct):
                    v.wait_ge(g.qG.h, qg_ct)
                    if (tb - 1, 1) in dv_cadd:
                        v.wait_ge(g.sPE.h, pe_tr_hist.get(tb - 1, 0))
                    if CMB_MODE == "pure":
                        v.drain().then_inc(g.sDV.h, 1)
                    else:
                        v.tensor_scalar(ctf[:], ctile[:], csc_str[:, tb:tb + 1], None,
                                        ALU.mult).then_inc(g.sDV.h, 1)
                    g.sDV.inc()
                    dv_cmb[tb] = g.sDV.v
                block.vector(_)
                dv_cf = g.sDV.v
                if CMB_MODE in ("gather", "scale", "pure", "plaindma"):
                    continue

                for half in range(2):
                    def _(t, tb=tb, half=half, dv_cf=dv_cf):
                        t.wait_ge(g.sDV.h, dv_cf)
                        if (tb, half) != (0, 0) and (tb - 1, half) in dv_cadd:
                            t.wait_ge(g.sDV.h, dv_cadd[(tb - 1, half)])
                        for k in range(4):
                            dc = half * 4 + k
                            mm = t.matmul(ps_c[half][:, k * P:(k + 1) * P],
                                          ctf[:, dc * P:(dc + 1) * P],
                                          g.ident_fs[:, :], start=True, stop=True)
                        mm.then_inc(g.sPE.h, 1)
                        g.sPE.inc()
                        pe_tr_hist[tb] = g.sPE.v
                    block.tensor(_)
                    pe_c = g.sPE.v

                    def _(v, tb=tb, half=half, pe_c=pe_c):
                        v.wait_ge(g.sPE.h, pe_c)
                        if CMB_MODE == "tr":
                            v.tensor_copy(ctf[:, 0:P], ps_c[half][:, 0:P]).then_inc(g.sDV.h, 1)
                        else:
                            for k in range(4):
                                dc = half * 4 + k
                                mm = v.tensor_tensor(
                                    g.X[:, dc * T + tb * P:dc * T + (tb + 1) * P],
                                    g.X[:, dc * T + tb * P:dc * T + (tb + 1) * P],
                                    ps_c[half][:, k * P:(k + 1) * P], ALU.add)
                            mm.then_inc(g.sDV.h, 1)
                        g.sDV.inc()
                        dv_cadd[(tb, half)] = g.sDV.v
                    block.vector(_)


def emit_l1attn(g):
    raise NotImplementedError



def kernel(**inputs):
    shared = prep_shared(inputs)
    in_maps = [prep_core(inputs, shared, c) for c in range(NCORES)]
    nc = build_program(N_SUB)
    res = run_bass_kernel_spmd(nc, in_maps, core_ids=list(range(NCORES)))
    outs = [np.ascontiguousarray(res.results[c]["out_fm"].T) for c in range(NCORES)]
    return np.concatenate(outs, axis=0).reshape(2, 8192, 1024).astype(np.float32)
